# version 15
# baseline (speedup 1.0000x reference)
import numpy as np
import ml_dtypes
from contextlib import ExitStack

import concourse.bass as bass
import concourse.mybir as mybir

F32 = mybir.dt.float32
BF16 = mybir.dt.bfloat16
AF = mybir.ActivationFunctionType
ALU = mybir.AluOpType
AX = mybir.AxisListType
BF = ml_dtypes.bfloat16

ENGINES = ("tensor", "scalar", "vector", "gpsimd", "sync")
NT = 16
T = 2048
D = 1024
NP = 8
DH = 64
DFF = 4096


class Prog:
    def __init__(self, nc):
        self.nc = nc
        self.plan = {e: [] for e in ENGINES}
        self.count = {}
        self.lastwait = {e: {} for e in ENGINES}
        self.dma_sems = set()
        self.sems = {}

    def op(self, eng, meth, *args, wait=(), inc=False, dsem=None, **kw):
        is_dma = meth in ("dma_start", "dma_start_transpose")
        sname = (dsem or (eng + "_d")) if is_dma else eng
        if is_dma:
            self.dma_sems.add(sname)
        waits = []
        for ev in wait:
            if ev is None:
                continue
            p, v = ev
            if v > self.lastwait[eng].get(p, 0):
                if p in self.dma_sems and v < self.count.get(p, 0):
                    raise RuntimeError(
                        f"stale DMA event: {eng} waiting ({p},{v}) but {p} already "
                        f"at {self.count[p]} — only latest event per sem is safe")
                if p == sname and v > self.count.get(sname, 0):
                    raise RuntimeError(f"{eng} waits own future {v} on {p}")
                waits.append((p, v))
                self.lastwait[eng][p] = v
        ev_out = None
        amt = 0
        if inc:
            amt = 16 if is_dma else 1
            self.count[sname] = self.count.get(sname, 0) + amt
            ev_out = (sname, self.count[sname])
        if waits or meth is not None:
            self.plan[eng].append((waits, meth, args, kw, amt, sname))
        return ev_out

    def snapshot(self):
        return [(s, v) for s, v in self.count.items() if v > 0 and s in ENGINES]

    def barrier(self, engines=ENGINES):
        evs = [self.op(e, "drain", fusable=False, inc=True) for e in engines]
        for e in engines:
            self.op(e, None, wait=evs)

    def emit(self):
        nc = self.nc
        names = set(self.count) | {w[0] for e in ENGINES
                                   for (ws, *_r) in self.plan[e] for w in ws}
        with ExitStack() as es:
            for s in sorted(names):
                self.sems[s] = es.enter_context(nc.semaphore("sem_" + s))
            with nc.Block() as block:
                for e in ENGINES:
                    getattr(block, e)(lambda h, e=e: self._run(e, h))

    def _run(self, eng, h):
        sems = self.sems
        for waits, meth, args, kw, amt, sname in self.plan[eng]:
            for (p, v) in waits:
                h.wait_ge(sems[p], v)
            if meth is None:
                continue
            inst = getattr(h, meth)(*args, **kw)
            if amt:
                inst.then_inc(sems[sname], amt)


def ap(t, dims, off=0):
    return bass.AP(t, off, [list(d) for d in dims])


def host_inputs(inputs, core, nseq=2):
    f = np.float32
    d = {}
    x = np.asarray(inputs["L0_attn_a_x"], dtype=f)
    d["x_in"] = np.ascontiguousarray(x[2 * core:2 * core + nseq])

    wqkv = np.asarray(inputs["L0_attn_a_w_qkv"], dtype=f)
    pair_blocks = lambda w: np.ascontiguousarray(
        w.reshape(D, NP, 128).transpose(1, 0, 2).reshape(NP * D, 128))
    rhs_blocks = lambda w: np.ascontiguousarray(
        w.reshape(w.shape[0] // 128, 128, w.shape[1] // 512, 512)
        .transpose(0, 2, 1, 3).reshape(-1, 512))
    d["wq_a"] = pair_blocks(wqkv[:, 0:D])
    d["wk_a"] = pair_blocks(wqkv[:, D:2 * D])
    d["wv_a"] = pair_blocks(np.ascontiguousarray(wqkv[:, 2 * D:3 * D]))
    d["w_out"] = rhs_blocks(np.asarray(inputs["L0_attn_a_w_out"], dtype=f))

    w1 = np.asarray(inputs["L0_ffn_w1"], dtype=f)
    d["w1r"] = np.ascontiguousarray(
        w1.reshape(D, 32, 128).transpose(1, 0, 2).reshape(32 * D, 128))
    d["w2r"] = rhs_blocks(np.asarray(inputs["L0_ffn_w2"], dtype=f))
    b1 = np.asarray(inputs["L0_ffn_b1"], dtype=f)
    d["b1_t"] = np.ascontiguousarray(b1.reshape(32, 128).T)
    d["b2_r"] = np.ascontiguousarray(np.asarray(inputs["L0_ffn_b2"], dtype=f).reshape(1, D))

    d["wq_b"] = pair_blocks(np.asarray(inputs["L0_attn_b_Wq"], dtype=f))
    d["wkvd"] = np.ascontiguousarray(np.asarray(inputs["L0_attn_b_Wkv_down"], dtype=f))
    wkvu = np.asarray(inputs["L0_attn_b_Wkv_up"], dtype=f).reshape(256, 16, 2, DH)
    ku = wkvu[:, :, 0, :].reshape(256, NP, 128).transpose(1, 0, 2)
    vu = wkvu[:, :, 1, :].reshape(256, NP, 128).transpose(1, 0, 2)
    d["wkvu_k"] = np.ascontiguousarray(ku.reshape(NP * 256, 128))
    d["wkvu_v"] = np.ascontiguousarray(vu.reshape(NP * 256, 128))
    d["wo_b"] = rhs_blocks(np.asarray(inputs["L0_attn_b_Wo"], dtype=f))
    d["bq_t"] = np.ascontiguousarray(np.asarray(inputs["L0_attn_b_bq"], dtype=f).reshape(8, 128).T)
    d["bkvd_t"] = np.ascontiguousarray(
        np.asarray(inputs["L0_attn_b_bkv_down"], dtype=f).reshape(2, 128).T)
    bku = np.asarray(inputs["L0_attn_b_bkv_up"], dtype=f).reshape(16, 2, DH)
    d["bku_k_t"] = np.ascontiguousarray(bku[:, 0, :].reshape(8, 128).T)
    d["bku_v_r"] = np.ascontiguousarray(bku[:, 1, :].reshape(1, D))
    d["bo_r"] = np.ascontiguousarray(np.asarray(inputs["L0_attn_b_bo"], dtype=f).reshape(1, D))

    d["ident"] = np.eye(128, dtype=BF)
    d["ident32"] = np.eye(128, dtype=f)
    rp = np.zeros((128, 128), dtype=f)
    for m in range(128):
        rp[(m // 64) * 64 + (m + 32) % 64, m] = 1.0
    d["rotp"] = rp
    idx = np.arange(T, dtype=np.float64)
    delta = np.arange(-(T - 1), T, dtype=np.float64)
    ftvec = (-0.1 * np.log(np.abs(delta) + 1.0) + np.log(
        np.exp(-np.maximum(delta, 0.0) * 0.01) + 1e-9)).astype(f)
    d["ftp"] = np.ascontiguousarray(
        np.stack([ftvec[127 - j: 127 - j + 3968] for j in range(128)]))
    inv = 10000.0 ** (-np.arange(0, 32, dtype=np.float64) / 32.0)
    fr = np.outer(idx, inv)
    cos = np.cos(fr).T.astype(f)
    sin = np.sin(fr).T.astype(f)
    d["ropeC"] = np.ascontiguousarray(np.tile(cos, (4, 1)))
    d["ropeS"] = np.ascontiguousarray(np.concatenate([-sin, sin, -sin, sin], axis=0))
    return d


INPUT_SPECS = {
    "x_in": (None, F32),
    "wq_a": ([NP * D, 128], F32), "wk_a": ([NP * D, 128], F32),
    "wv_a": ([NP * D, 128], F32), "w_out": ([16 * 128, 512], F32),
    "w1r": ([32 * D, 128], F32), "b1_t": ([128, 32], F32),
    "w2r": ([64 * 128, 512], F32), "b2_r": ([1, D], F32),
    "wq_b": ([NP * D, 128], F32), "wkvd": ([D, 256], F32),
    "wkvu_k": ([NP * 256, 128], F32), "wkvu_v": ([NP * 256, 128], F32),
    "wo_b": ([16 * 128, 512], F32),
    "bq_t": ([128, 8], F32), "bkvd_t": ([128, 2], F32), "bku_k_t": ([128, 8], F32),
    "bku_v_r": ([1, D], F32), "bo_r": ([1, D], F32),
    "ident": ([128, 128], BF16), "ident32": ([128, 128], F32),
    "rotp": ([128, 128], F32), "ftp": ([128, 3968], F32),
    "ropeC": ([128, T], F32), "ropeS": ([128, T], F32),
}


class MK:
    def __init__(self, sublayers=("attn_a", "ffn", "attn_b", "ffn"), nseq=2):
        self.sublayers = sublayers
        self.nseq = nseq
        nc = self.nc = bass.Bass("TRN2", target_bir_lowering=False)
        self.p = Prog(nc)
        self.es = ExitStack()
        self.d = {}
        for name, (shape, dt) in INPUT_SPECS.items():
            if name == "x_in":
                shape = [nseq, T, D]
            self.d[name] = nc.dram_tensor(name, shape, dt, kind="ExternalInput")
        self.d["x_out"] = nc.dram_tensor("x_out", [nseq, T, D], F32, kind="ExternalOutput")
        if "attn_a" in sublayers:
            self.d["vp_d"] = nc.dram_tensor("vp_d", [128, 2080], BF16, kind="Internal")
            self.d["od_d"] = nc.dram_tensor("od_d", [128, 2048], BF16, kind="Internal")

        sb = lambda name, shape, dt: self.es.enter_context(nc.sbuf_tensor(name, shape, dt))
        self.x_sb = sb("x_sb", [128, NT * D], F32)
        self.xh_fm = sb("xh_fm", [128, 8 * T], BF16)
        self.wb = [sb(f"wb{i}", [128, 3072], BF16) for i in range(2)]
        self.ident_sb = sb("ident_sb", [128, 128], BF16)
        self.ident32_sb = sb("ident32_sb", [128, 128], F32)
        self.rotp_sb = sb("rotp_sb", [128, 128], F32)
        self.ones_sb = sb("ones_sb", [128, 128], BF16)
        self.xt32 = sb("xt32", [128, 1024], F32)
        self.eps_sb = sb("eps_sb", [128, 1], F32)
        self.ln_sum = sb("ln_sum", [128, NT], F32)
        self.ln_sq = sb("ln_sq", [128, NT], F32)
        self.ln_m = sb("ln_m", [128, NT], F32)
        self.ln_m2 = sb("ln_m2", [128, NT], F32)
        self.ln_var = sb("ln_var", [128, NT], F32)
        self.ln_rs = sb("ln_rs", [128, NT], F32)
        self.ln_b = sb("ln_b", [128, NT], F32)
        self.b1_sb = sb("b1_sb", [128, 32], F32)
        self.bq_sb = sb("bq_sb", [128, 8], F32)
        self.bq8_sb = sb("bq8_sb", [128, 8], F32)
        self.bkvd_sb = sb("bkvd_sb", [128, 2], F32)
        self.bkuk_sb = sb("bkuk_sb", [128, 8], F32)
        self.bkuv_bf = sb("bkuv_bf", [1, D], BF16)
        self.bo_bf = sb("bo_bf", [1, D], BF16)
        self.PS = self.es.enter_context(nc.psum_tensor("PS", [128, 4096], F32))
        self._scopes = []
        self.prefetch = {}
        self.prefetch_ffn = {}
        self.ln_pend = None
        self.ln_e8 = None
        self.ln_stat_prev = None

    def push_scope(self):
        self._scopes.append(ExitStack())

    def pop_scope(self):
        self._scopes.pop().close()

    def ssb(self, name, shape, dt):
        self._uid = getattr(self, "_uid", 0) + 1
        return self._scopes[-1].enter_context(
            self.nc.sbuf_tensor(f"{name}_{self._uid}", shape, dt))

    def alloc_attn_common(self):
        ent = self.p.snapshot()
        self.obuf = self.ssb("obuf", [65, 1024], F32)
        self.ev_obuf_war = list(ent)
        self.ev_cast_war = list(ent)
        self.ev_of32_war = list(ent)
        self.ofm = self.ssb("ofm", [128, 8 * T], BF16)
        self.qf = [self.ssb(f"qf{i}", [128, T], BF16) for i in range(2)]
        self.kf = [self.ssb(f"kf{i}", [128, T], BF16) for i in range(2)]
        self.vpair = [self.ssb(f"vp{i}", [128, 16 * 130], BF16) for i in range(2)]
        self.of32 = self.ssb("of32", [128, T], F32)
        self.pt = [self.ssb(f"pt{i}", [128, 1024], BF16) for i in range(2)]
        self.smax = self.ssb("smax", [128, 16], F32)
        self.smax2 = self.ssb("smax2", [128, 2], F32)
        self.sa1b = self.ssb("sa1b", [1, 2], BF16)
        self.nega = self.ssb("nega", [128, 2], F32)
        self.rz = self.ssb("rz", [128, 16], F32)
        self.pair_done = [list(ent), list(ent)]
        self.prep_last = list(ent)
        self.attn_prev0 = list(ent)
        self.attn_prev1 = list(ent)
        self.obank_war = list(ent)
        self.prev_o = [None, None]
        self.wbl = [list(ent), list(ent)]
        self.smax_war = [list(ent), list(ent)]
        self.smax2_war = [list(ent), list(ent)]
        self.sa1b_war = [list(ent), list(ent)]
        self.rope_last = list(ent)
        self.prep_out = {}
        self._bg = []
        self._bg_skip = 0

    def ps(self, c0, n):
        return self.PS[:, c0:c0 + n]

    def dma_in(self, dst_ap, dram, row0, nrows, ncols, eng="sync", wait=(), inc=True):
        src = ap(dram, [[ncols, nrows], [1, ncols]], off=row0 * ncols)
        return self.p.op(eng, "dma_start", dst_ap, src, wait=wait, inc=inc)

    def stream_block(self, dram, row0, nk, ncols, buf, eng, dsem, wait=(), col0=0):
        src = ap(dram, [[ncols, 128], [128 * ncols, nk], [1, ncols]], off=row0 * ncols)
        dst = ap(buf, [[buf.shape[1], 128], [ncols, nk], [1, ncols]], off=col0)
        return self.p.op(eng, "dma_start", dst, src, wait=wait, dsem=dsem, inc=True)

    def setup(self):
        p, d = self.p, self.d
        ev = {}
        for name, dst in (("ident", self.ident_sb), ("ident32", self.ident32_sb),
                          ("rotp", self.rotp_sb),
                          ("b1", self.b1_sb), ("bq", self.bq_sb),
                          ("bkvd", self.bkvd_sb), ("bkuk", self.bkuk_sb)):
            fence = p.op("sync", "dma_start", dst[:],
                         d[{"ident": "ident", "ident32": "ident32", "b1": "b1_t",
                            "rotp": "rotp",
                            "bq": "bq_t", "bkvd": "bkvd_t",
                            "bkuk": "bku_k_t"}[name]].ap(), dsem="setup", inc=True)
        for name, dst, srcn in (("bkuv", self.bkuv_bf, "bku_v_r"),
                                ("bo", self.bo_bf, "bo_r")):
            gfence = p.op("gpsimd", "dma_start", dst[:], d[srcn].ap(),
                          dsem="setupg", inc=True)
        for k in ("ident", "ident32", "rotp", "b1", "bq", "bkvd", "bkuk"):
            ev[k] = fence
        for k in ("bkuv", "bo"):
            ev[k] = gfence
        ev["ones"] = p.op("gpsimd", "memset", self.ones_sb[:], 1.0, inc=True)
        ev["bq8"] = p.op("vector", "tensor_scalar", self.bq8_sb[:], self.bq_sb[:],
                         0.125, None, ALU.mult, wait=[ev["bq"]], inc=True)
        ev["eps"] = p.op("gpsimd", "memset", self.eps_sb[:], 1e-5, inc=True)
        self.ev_prep_final = None
        self.ev_setup = ev

    def load_x(self, s, extra_wait=()):
        p, d = self.p, self.d
        x = d["x_in"]
        evs = [None] * 4
        for t in range(NT):
            evs[t % 4] = p.op("sync", "dma_start",
                              self.x_sb[:, t * D:(t + 1) * D],
                              ap(x, [[D, 128], [1, D]],
                                 off=(s * T + t * 128) * D),
                              wait=extra_wait, dsem=f"io{t % 4}", inc=True)
        return evs

    def store_x(self, s, ev_ready):
        p, d = self.p, self.d
        x = d["x_out"]
        for t in range(NT):
            dst = ap(x, [[D, 128], [1, D]], off=(s * T + t * 128) * D)
            e = p.op("sync", "dma_start", dst, self.x_sb[:, t * D:(t + 1) * D],
                     wait=[ev_ready[t]], dsem="io", inc=True)
        return [e]

    def ln_stats(self, t, ev):
        p = self.p
        xt = self.x_sb[:, t * D:(t + 1) * D]
        e1 = p.op("scalar", "activation", self.xt32[:, 0:D], xt, AF.Copy,
                  accum_out=self.ln_sum[:, t:t + 1],
                  wait=[ev, self.ln_e8, self.ln_stat_prev], inc=True)
        e2 = p.op("scalar", "activation", self.xt32[:, 0:D], xt, AF.Square,
                  accum_out=self.ln_sq[:, t:t + 1], wait=[e1], inc=True)
        self.ln_stat_prev = e2
        self.ln_pend = [e2]
        return e2

    def layernorm(self, ev_x):
        p = self.p
        if self.ln_pend is not None:
            ev_stats = self.ln_pend
            self.ln_pend = None
        else:
            ev_stats = []
            for t in range(NT):
                xt = self.x_sb[:, t * D:(t + 1) * D]
                e1 = p.op("vector", "reduce_sum", self.ln_sum[:, t:t + 1], xt,
                          axis=AX.X, wait=[ev_x[t], self.ln_e8], inc=True)
                e2 = p.op("vector", "scalar_tensor_tensor", self.xt32[:, 0:D],
                          xt, 1.0, xt,
                          ALU.mult, ALU.mult, accum_out=self.ln_sq[:, t:t + 1],
                          wait=[e1], inc=True)
                ev_stats.append(e2)
        e3 = p.op("vector", "tensor_scalar", self.ln_m[:], self.ln_sum[:], 1.0 / D, None,
                  ALU.mult, wait=ev_stats, inc=True)
        e4 = p.op("vector", "tensor_tensor", self.ln_m2[:], self.ln_m[:], self.ln_m[:],
                  ALU.mult, wait=[e3], inc=True)
        e5 = p.op("vector", "scalar_tensor_tensor", self.ln_var[:], self.ln_sq[:], 1.0 / D,
                  self.ln_m2[:], ALU.mult, ALU.subtract, wait=[e4], inc=True)
        e6 = p.op("scalar", "activation", self.ln_var[:], self.ln_var[:], AF.Ln,
                  bias=self.eps_sb[:], wait=[e5, self.ev_setup["eps"]], inc=True)
        e7 = p.op("scalar", "activation", self.ln_rs[:], self.ln_var[:], AF.Exp,
                  scale=-0.5, wait=[e6], inc=True)
        e8 = p.op("vector", "scalar_tensor_tensor", self.ln_b[:], self.ln_m[:], -1.0,
                  self.ln_rs[:], ALU.mult, ALU.mult, wait=[e7], inc=True)
        self.ln_e8 = e8
        ev_out = []
        ev_evac_par = [None, None]
        etr_par = [None, None]
        self.push_scope()
        xbuf = [self.xt32, self.ssb("ln_x2", [128, 1024], F32)]
        for t in range(NT):
            xt = self.x_sb[:, t * D:(t + 1) * D]
            xb = xbuf[t % 2]
            en = p.op("vector", "tensor_scalar", xb[:, 0:D], xt,
                      self.ln_rs[:, t:t + 1], self.ln_b[:, t:t + 1], ALU.mult, ALU.add,
                      wait=[e8, etr_par[t % 2]], inc=True)
            po = 2048 * (t % 2)
            etr = None
            for f in range(8):
                etr = p.op("tensor", "transpose",
                           self.ps(po, 1024)[:, f * 128:(f + 1) * 128],
                           xb[:, f * 128:(f + 1) * 128], self.ident32_sb[:],
                           wait=[en, ev_evac_par[t % 2], self.ev_setup["ident32"]],
                           inc=(f == 7))
            etr_par[t % 2] = etr
            dst = ap(self.xh_fm, [[8 * T, 128], [T, 8], [1, 128]], off=t * 128)
            ev_evac_par[t % 2] = p.op("scalar", "activation", dst,
                                      self.ps(po, 1024)[:], AF.Copy,
                                      wait=[etr], inc=True)
            ev_out.append(ev_evac_par[t % 2])
        self.pop_scope()
        return ev_out

    def _emit_prefetch(self, nxt):
        p, d = self.p, self.d
        self.prefetch = {}
        if nxt == "attn_a":
            ev = self.stream_block(d["wv_a"], 0, 8, 128, self.wb[0], "gpsimd",
                                   "pfv", wait=list(self.entry), col0=2048)
            self.prefetch = {"sl": "attn_a", "wv": ev}
        elif nxt == "attn_b":
            ek = self.stream_block(d["wkvu_k"], 0, 2, 128, self.wb[0],
                                   "gpsimd", "pfk", wait=list(self.entry),
                                   col0=2048)
            ev = self.stream_block(d["wkvu_v"], 0, 2, 128, self.wb[0],
                                   "gpsimd", "pfv", col0=2304)
            self.prefetch = {"sl": "attn_b", "wk": ek, "wv": ev}

    def ffn(self, ev_ln, nxt=None, stats_ok=False):
        p, d = self.p, self.d
        self.push_scope()
        hbuf = self.ssb("hbuf", [128, 32 * 1024], BF16)
        b2_bf = self.ssb("b2_bf", [1, D], BF16)
        ev_b2 = p.op("gpsimd", "dma_start", b2_bf[:], d["b2_r"].ap(),
                     wait=list(self.entry), dsem="setupg", inc=True)
        w1 = d["w1r"]
        w2 = d["w2r"]
        ev_res = [None] * NT
        ev_res_last = None
        wsl = [None] * 4
        first = list(self.entry)
        for half in range(2):
            tok0 = half * 1024
            ev_h = [None] * 32
            for m in range(32):
                if half == 0 and m == 4:
                    self._emit_prefetch(nxt)
                s = m % 4
                bi, c0 = s // 2, (s % 2) * 1024
                if half == 0 and m < 2 and self.prefetch_ffn:
                    ev_w = self.prefetch_ffn[f"w{m}"]
                else:
                    ev_w = self.stream_block(w1, m * D, 8, 128, self.wb[bi],
                                             "gpsimd", f"wsl{s}", col0=c0,
                                             wait=[wsl[s]]
                                             + (first if half == 0 and m < 4
                                                else []))
                pp = self.ps(2048 * (m % 2), 1024)
                emm = None
                for k in range(8):
                    lhs = self.wb[bi][:, c0 + k * 128: c0 + k * 128 + 128]
                    for n in range(2):
                        rhs = self.xh_fm[:, k * T + tok0 + n * 512:
                                         k * T + tok0 + n * 512 + 512]
                        emm = p.op("tensor", "matmul", pp[:, n * 512:n * 512 + 512],
                                   lhs, rhs, start=(k == 0), stop=(k == 7),
                                   wait=[ev_w, ev_ln[-1],
                                         ev_h[m - 2] if m >= 2 else None,
                                         ev_res_last],
                                   inc=(k == 7 and n == 1))
                wsl[s] = emm
                ev_h[m] = p.op("scalar", "activation", hbuf[:, m * 1024:(m + 1) * 1024],
                               pp[:], AF.Gelu, bias=self.b1_sb[:, m:m + 1],
                               wait=[emm, self.ev_setup["b1"]]
                               + (self.entry if (half == 0 and m < 2) else []),
                               inc=True)
            ev_sweep_res = []
            for n in range(2):
                emm_t = [None] * 8
                for c in range(32):
                    s = c % 4
                    bi, c0 = s // 2, (s % 2) * 1024
                    ev_w2 = self.stream_block(w2, (c * 2 + n) * 128, 1, 512,
                                              self.wb[bi], "gpsimd", f"wsl{s}",
                                              col0=c0, wait=[wsl[s]])
                    for t8 in range(8):
                        lhs = hbuf[:, c * 1024 + t8 * 128: c * 1024 + t8 * 128 + 128]
                        pp = self.ps(512 * t8, 512)
                        w0 = [ev_w2, ev_h[c]]
                        if c == 0:
                            w0 += [ev_h[31]] + ev_sweep_res
                        emm_t[t8] = p.op(
                            "tensor", "matmul", pp, lhs,
                            self.wb[bi][:, c0:c0 + 512],
                            start=(c == 0), stop=False, wait=w0,
                            inc=(t8 == 7))
                    wsl[s] = emm_t[7]
                ev_sweep_res = []
                for t8 in range(8):
                    pp = self.ps(512 * t8, 512)
                    eb = p.op("tensor", "matmul", pp, self.ones_sb[0:1, :],
                              b2_bf[:, n * 512:n * 512 + 512],
                              start=False, stop=True,
                              wait=[emm_t[t8], ev_b2, self.ev_setup["ones"]],
                              inc=True)
                    t = half * 8 + t8
                    xt = self.x_sb[:, t * D + n * 512: t * D + n * 512 + 512]
                    er = p.op("vector", "tensor_tensor", xt, xt, pp, ALU.add,
                              wait=[eb, ev_res[t] if n == 1 else None], inc=True)
                    if n == 1 and stats_ok:
                        self.ln_stats(t, er)
                    ev_sweep_res.append(er)
                    ev_res[t] = er
                    ev_res_last = er
        self.prefetch_ffn = {}
        self.pop_scope()
        return ev_res


    def _bg_pump(self, n=1):
        while n > 0 and self._bg:
            if self._bg_skip > 0:
                self._bg_skip -= 1
                return
            self._bg.pop(0)()
            n -= 1

    def _bg_drain(self):
        while self._bg:
            self._bg.pop(0)()

    def _sa_new_state(self, ev0, ev1, pi, prev_o):
        return {"exp": [list(ev0), list(ev1)],
                "o_by_par": list(prev_o),
                "dance": None,
                "jpar": 0, "pi": pi}

    def _sa_pipeline(self, st, jlist, score_fn, o_fn, use_wt, wtb, wt_tile_fn,
                    first_o_waits, post_o=None):
        p = self.p
        nj = len(jlist)
        pi = st["pi"]
        pend = []
        for idx in range(nj + 1):
            if idx < nj:
                j = jlist[idx]
                par = st["jpar"]
                st["jpar"] ^= 1
                if use_wt:
                    eb = None
                    for hl in range(2):
                        eb = p.op("tensor", "matmul",
                                  self.ps(1024 * par + 512 * hl, 512),
                                  self.ident_sb[:], wt_tile_fn(j),
                                  start=True, stop=False,
                                  wait=[self.ev_ftw, self.ev_setup["ident"]]
                                       + st["exp"][0], inc=(hl == 1))
                    ev_s = score_fn(j, 1024 * par, [eb])
                else:
                    ev_s = score_fn(j, 1024 * par, list(st["exp"][0]))
                ev_e = p.op("scalar", "activation", self.pt[par][:],
                            self.ps(1024 * par, 1024), AF.Exp,
                            bias=self.nega[:, pi:pi + 1],
                            wait=[ev_s, st["o_by_par"][par]], inc=True)
                st["exp"] = [st["exp"][1], [ev_e]]
                gate = ev_e
                pend.append((idx, j, par, gate))
            self._bg_pump(1)
            if idx >= 1:
                (oidx, oj, opar, ogate) = pend.pop(0)
                extra = list(first_o_waits) if oidx == 0 else []
                evo = o_fn(oj, opar, ogate, extra, oidx == 0, oidx == nj - 1)
                st["o_by_par"][opar] = evo
                if post_o is not None:
                    post_o(oidx, oj, evo)
        return evo

    def o_dance(self, W, ev_oA, ev_oB):
        p = self.p
        nt = W // 128
        self._bg_pump(3)
        if self.dance_eng == "scalar":
            eevA = p.op("scalar", "activation", self.obuf[0:65, 0:W],
                        self.ps(2048, W)[0:65, :], AF.Copy,
                        wait=[ev_oA] + self.ev_obuf_war, inc=True)
            eevB = p.op("scalar", "activation", self.obuf[0:65, W:2 * W],
                        self.ps(2560, W)[0:65, :], AF.Copy,
                        wait=[ev_oB], inc=True)
        else:
            eevA = p.op("vector", "tensor_copy", self.obuf[0:65, 0:W],
                        self.ps(2048, W)[0:65, :],
                        wait=[ev_oA] + self.ev_obuf_war, inc=True)
            eevB = p.op("vector", "tensor_copy", self.obuf[0:65, W:2 * W],
                        self.ps(2560, W)[0:65, :], wait=[ev_oB, eevA], inc=True)
        etA = etB = None
        ident65 = self.ident32_sb[0:65, 0:65]
        for t in range(nt):
            etA = p.op("tensor", "transpose", self.ps(2048 + t * 128, 65),
                       self.obuf[0:65, t * 128:(t + 1) * 128], ident65,
                       wait=[eevA, eevB], inc=(t == nt - 1))
        for t in range(nt):
            etB = p.op("tensor", "transpose", self.ps(2560 + t * 128, 65),
                       self.obuf[0:65, W + t * 128: W + (t + 1) * 128],
                       ident65, wait=[eevB], inc=(t == nt - 1))
        self.ev_obuf_war = [etB]
        return etA, etB

    def _evac_of32(self, dst, dst_off, ntile, etA, etB, prev_read):
        p, psap = self.p, lambda off, dims: ap(self.PS, dims, off=off)
        erzA = p.op("vector", "reciprocal", self.rz[:, 0:ntile],
                    psap(2048 + 64, [[4096, 128], [128, ntile], [1, 1]]),
                    wait=[etA], inc=True)
        erzB = p.op("vector", "reciprocal", self.rz[:, 8:8 + ntile],
                    psap(2560 + 64, [[4096, 128], [128, ntile], [1, 1]]),
                    wait=[etB], inc=True)
        out = []
        for hl, (erz_h, po, rz0) in enumerate(((erzA, 2048, 0), (erzB, 2560, 8))):
            e = p.op("vector", "tensor_tensor",
                     ap(dst, [[dst.shape[1], 128], [128, ntile], [1, 64]],
                        off=dst_off + 64 * hl),
                     psap(po, [[4096, 128], [128, ntile], [1, 64]]),
                     ap(self.rz, [[16, 128], [1, ntile], [0, 64]], off=rz0),
                     ALU.mult, wait=[erz_h] + list(prev_read), inc=True)
            out.append(e)
        return out

    def _prep_bank(self, pp):
        pp["bk"] ^= 1
        base = 3072 + 512 * pp["bk"]
        war = pp["bank_ev"][pp["bk"]]
        return base, war

    def _stats_steps(self, steps, pp, out, pr, qf, kf):
        p = self.p
        pi = pr % 2

        def s_stat(sidx):
            hl, n = sidx // 4, sidx % 4
            b0 = 64 * hl
            base, war = self._prep_bank(pp)
            e = p.op("tensor", "matmul", self.ps(base, 512),
                     kf[b0:b0 + 64, 0:2048:16],
                     qf[b0:b0 + 64, n * 512:n * 512 + 512],
                     start=True, stop=True, tile_position=(b0, 0),
                     wait=[out["qe"], out["ke"]] + war, inc=True)
            er = p.op("vector", "reduce_max",
                      self.smax[:, 8 * pi + sidx: 8 * pi + sidx + 1],
                      self.ps(base, 512), axis=AX.X,
                      wait=[e] + self.smax_war[pi], inc=True)
            pp["bank_ev"][pp["bk"]] = [er]
            pp["st8"] = er
        for sidx in range(8):
            steps.append(lambda sidx=sidx: s_stat(sidx))

        def s_comb():
            er2 = p.op("vector", "reduce_max", self.smax2[:, pi:pi + 1],
                       self.smax[:, 8 * pi:8 * pi + 8], axis=AX.X,
                       wait=[pp["st8"]] + self.smax2_war[pi], inc=True)
            self.smax_war[pi] = [er2]
            base, war = self._prep_bank(pp)
            etr = p.op("tensor", "transpose", self.ps(base, 128)[0:1, :],
                       self.smax2[:, pi:pi + 1], self.ident32_sb[:],
                       wait=[er2, self.ev_setup["ident32"]] + war, inc=True)
            erow = p.op("vector", "reduce_max", self.smax2[0:1, pi:pi + 1],
                        self.ps(base, 128)[0:1, :], axis=AX.X,
                        wait=[etr], inc=True)
            ecb = p.op("vector", "tensor_copy", self.sa1b[:, pi:pi + 1],
                       self.smax2[0:1, pi:pi + 1],
                       wait=[erow] + self.sa1b_war[pi], inc=True)
            self.smax2_war[pi] = [etr, ecb]
            ebc = p.op("tensor", "matmul", self.ps(base + 128, 1)[0:128, 0:1],
                       self.ones_sb[0:1, :], self.sa1b[:, pi:pi + 1],
                       start=True, stop=True,
                       wait=[ecb, self.ev_setup["ones"]], inc=True)
            self.sa1b_war[pi] = [ebc]
            ev_nega = p.op("scalar", "activation", self.nega[:, pi:pi + 1],
                           self.ps(base + 128, 1)[:, 0:1], AF.Copy, scale=-1.0,
                           wait=[ebc] + self.pair_done[pi], inc=True)
            pp["bank_ev"][pp["bk"]] = [ev_nega]
            self.prep_last = [ev_nega]
            out["nega"] = ev_nega
            out["erow"] = erow
        steps.append(s_comb)

    def _prep_steps_a(self, pr, evl):
        p, d = self.p, self.d
        pi = pr % 2
        wbuf = self.wb[pi]
        qf, kf, vpair = self.qf[pi], self.kf[pi], self.vpair[pi]
        pp = {"bk": 0, "bank_ev": [list(self.prep_last), list(self.prep_last)]}
        out = {}
        self.prep_out[pr] = out
        steps = []

        pf = (pr == 0 and self.prefetch.get("sl") == "attn_a")

        def s_stream():
            if pf:
                pp["wv"] = self.prefetch["wv"]
                pp["wq"] = self.stream_block(d["wq_a"], pr * D, 8, 128, wbuf,
                                             "gpsimd", f"gwq{pi}",
                                             wait=self.wbl[pi]
                                             + self.pair_done[pi])
            else:
                pp["wv"] = self.stream_block(d["wv_a"], pr * D, 8, 128, wbuf,
                                             "gpsimd", f"gwv{pi}", col0=2048,
                                             wait=self.wbl[pi]
                                             + self.pair_done[pi])
                pp["wq"] = self.stream_block(d["wq_a"], pr * D, 8, 128, wbuf,
                                             "gpsimd", f"gwq{pi}")
            pp["wk"] = self.stream_block(d["wk_a"], pr * D, 8, 128, wbuf,
                                         "gpsimd", f"gwk{pi}", col0=1024)
            pp["ones"] = p.op("gpsimd", "memset", vpair[:], 1.0,
                              wait=self.pair_done[pi], inc=True)
        steps.append(s_stream)

        def s_v(t):
            base, war = self._prep_bank(pp)
            emv = None
            for k in range(8):
                emv = p.op("tensor", "matmul", self.ps(base, 128),
                           self.xh_fm[:, k * T + t * 128: k * T + t * 128 + 128],
                           wbuf[:, 2048 + k * 128: 2048 + k * 128 + 128],
                           start=(k == 0), stop=(k == 7),
                           wait=[pp["wv"], evl] + war, inc=(k == 7))
            dst = ap(vpair, [[2080, 128], [65, 2], [1, 64]], off=t * 130)
            srcp = ap(self.PS, [[4096, 128], [64, 2], [1, 64]], off=base)
            ev = p.op("vector", "tensor_scalar", dst, srcp, 1.0 / 3.0, None,
                      ALU.mult, wait=[emv, pp["ones"]], inc=True)
            pp["bank_ev"][pp["bk"]] = [ev]
            out["vdone"] = ev
        for t in range(NT):
            steps.append(lambda t=t: s_v(t))

        def s_q(n, col0, wev, dstbuf, scale, key):
            base, war = self._prep_bank(pp)
            emm = None
            for k in range(8):
                emm = p.op("tensor", "matmul", self.ps(base, 512),
                           wbuf[:, col0 + k * 128: col0 + k * 128 + 128],
                           self.xh_fm[:, k * T + n * 512: k * T + n * 512 + 512],
                           start=(k == 0), stop=(k == 7),
                           wait=[pp[wev], evl] + war, inc=(k == 7))
            ev = p.op("vector", "tensor_scalar",
                      dstbuf[:, n * 512:n * 512 + 512],
                      self.ps(base, 512), scale, None, ALU.mult,
                      wait=[emm] + self.pair_done[pi], inc=True)
            pp["bank_ev"][pp["bk"]] = [ev]
            self.wbl[pi] = [emm]
            out[key] = ev
        for n in range(4):
            steps.append(lambda n=n: s_q(n, 0, "wq", qf, 0.125, "qe"))
        for n in range(4):
            steps.append(lambda n=n: s_q(n, 1024, "wk", kf, 1.0, "ke"))

        self._stats_steps(steps, pp, out, pr, qf, kf)
        return steps

    def _prep_steps_b(self, pr, evl, latent, qtmp, ropec, ropes, rt0, ev_le):
        p, d = self.p, self.d
        pi = pr % 2
        wbuf = self.wb[pi]
        qf, kf, vpair = self.qf[pi], self.kf[pi], self.vpair[pi]
        pp = {"bk": 0, "bank_ev": [list(self.prep_last), list(self.prep_last)]}
        out = {}
        self.prep_out[pr] = out
        steps = []

        pf = (pr == 0 and self.prefetch.get("sl") == "attn_b")
        kc0, vc0 = (2048, 2304) if pf else (1024, 1280)

        def s_stream():
            if pf:
                pp["wk"] = self.prefetch["wk"]
                pp["wv"] = self.prefetch["wv"]
                pp["wq"] = self.stream_block(d["wq_b"], pr * D, 8, 128, wbuf,
                                             "gpsimd", f"gwq{pi}",
                                             wait=self.wbl[pi]
                                             + self.pair_done[pi])
            else:
                pp["wv"] = self.stream_block(d["wkvu_v"], pr * 256, 2, 128,
                                             wbuf, "gpsimd", f"gwv{pi}",
                                             col0=1280, wait=self.wbl[pi]
                                             + self.pair_done[pi])
                pp["wq"] = self.stream_block(d["wq_b"], pr * D, 8, 128, wbuf,
                                             "gpsimd", f"gwq{pi}")
                pp["wk"] = self.stream_block(d["wkvu_k"], pr * 256, 2, 128,
                                             wbuf, "gpsimd", f"gwk{pi}",
                                             col0=1024)
            pp["ones"] = p.op("gpsimd", "memset", vpair[:], 1.0,
                              wait=self.pair_done[pi], inc=True)
        steps.append(s_stream)

        def s_v(t):
            base, war = self._prep_bank(pp)
            for kc in range(2):
                p.op("tensor", "matmul", self.ps(base, 128),
                     latent[:, kc * T + t * 128: kc * T + t * 128 + 128],
                     wbuf[:, vc0 + kc * 128: vc0 + kc * 128 + 128],
                     start=(kc == 0), stop=False,
                     wait=[pp["wv"], ev_le] + war, inc=False)
            emv = p.op("tensor", "matmul", self.ps(base, 128),
                       self.ones_sb[0:1, :],
                       self.bkuv_bf[:, pr * 128: pr * 128 + 128],
                       start=False, stop=True,
                       wait=[self.ev_setup["bkuv"], self.ev_setup["ones"]],
                       inc=True)
            dst = ap(vpair, [[2080, 128], [65, 2], [1, 64]], off=t * 130)
            srcp = ap(self.PS, [[4096, 128], [64, 2], [1, 64]], off=base)
            ev = p.op("vector", "tensor_copy", dst, srcp,
                      wait=[emv, pp["ones"]], inc=True)
            pp["bank_ev"][pp["bk"]] = [ev]
            out["vdone"] = ev
        for t in range(NT):
            steps.append(lambda t=t: s_v(t))

        def s_qk_mm(n):
            baseq, warq = self._prep_bank(pp)
            emq = None
            for k in range(8):
                emq = p.op("tensor", "matmul", self.ps(baseq, 512),
                           wbuf[:, k * 128:(k + 1) * 128],
                           self.xh_fm[:, k * T + n * 512: k * T + n * 512 + 512],
                           start=(k == 0), stop=(k == 7),
                           wait=[pp["wq"], evl] + warq, inc=(k == 7))
            basek, wark = self._prep_bank(pp)
            emk = None
            for kc in range(2):
                emk = p.op("tensor", "matmul", self.ps(basek, 512),
                           wbuf[:, kc0 + kc * 128: kc0 + kc * 128 + 128],
                           latent[:, kc * T + n * 512: kc * T + n * 512 + 512],
                           start=(kc == 0), stop=(kc == 1),
                           wait=[pp["wk"], ev_le] + wark, inc=(kc == 1))
            pp["qk"] = (baseq, emq, basek, emk)
            self.wbl[pi] = [emq, emk]

        def s_tab(cc, wait):
            p.op("sync", "dma_start", ropec,
                 ap(d["ropeC"], [[T, 128], [1, 256]], off=cc * 256),
                 wait=wait, dsem="rtab", inc=True)
            return p.op("sync", "dma_start", ropes,
                        ap(d["ropeS"], [[T, 128], [1, 256]], off=cc * 256),
                        dsem="rtab", inc=True)

        def rope_args(which):
            baseq, emq, basek, emk = pp["qk"]
            if which == 0:
                return baseq, emq, qf, 0.125, self.bq8_sb[:, pr:pr + 1], (0, 1)
            return basek, emk, kf, 1.0, self.bkuk_sb[:, pr:pr + 1], (1, 0)

        def s_rope_ev(n, which):
            base, ev_mm, dst, sc, bias, subs = rope_args(which)
            prev = pp.get("rope_prev", self.rope_last) + self.pair_done[pi]
            pp["r_prev"] = prev
            bufs = {}
            ev0 = None
            for sub, buf in ((subs[0], qtmp), (subs[1], rt0)):
                ev0 = p.op("vector", "tensor_scalar", buf,
                           self.ps(base + sub * 256, 256), sc, bias,
                           ALU.mult, ALU.add,
                           wait=[ev_mm, ev0, self.ev_setup["bq8"],
                                 self.ev_setup["bkuk"]] + prev, inc=True)
                bufs[sub] = buf
            pp["r_bufs"] = bufs
            pp["r_ev0"] = ev0

        def s_rope_rot(n, which):
            base, ev_mm, dst, sc, bias, subs = rope_args(which)
            emr = None
            for i, sub in enumerate(subs):
                emr = p.op("tensor", "matmul", self.ps(base + sub * 256, 256),
                           self.rotp_sb[:], pp["r_bufs"][sub],
                           start=(i == 0), stop=(i == 1),
                           wait=[pp["r_ev0"], self.ev_setup["rotp"]],
                           inc=(i == 1))
            pp["r_rot"] = emr

        def s_rope_tt(n, which):
            base, ev_mm, dst, sc, bias, subs = rope_args(which)
            emr = pp["r_rot"]
            bufs = pp["r_bufs"]
            e3 = None
            for i, sub in enumerate(subs):
                cc = 2 * n + sub
                if which == 0 or i == 1:
                    ev_tab = s_tab(cc, [e3] if e3 else pp["r_prev"])
                    pp["tab"] = ev_tab
                e1 = p.op("vector", "tensor_tensor",
                          self.ps(base + sub * 256, 256),
                          self.ps(base + sub * 256, 256), ropes,
                          ALU.mult, wait=[emr, pp["tab"]], inc=True)
                e2 = p.op("vector", "tensor_tensor", bufs[sub],
                          bufs[sub], ropec, ALU.mult,
                          wait=[e1], inc=True)
                e3 = p.op("vector", "tensor_tensor",
                          dst[:, cc * 256:cc * 256 + 256],
                          self.ps(base + sub * 256, 256), bufs[sub],
                          ALU.add, wait=[e2] + self.pair_done[pi], inc=True)
            pp["rope_prev"] = [e3]
            self.rope_last = [e3]
            pp["bank_ev"][(base - 3072) // 512] = [e3]
            out["qe" if which == 0 else "ke"] = e3

        for n in range(4):
            steps.append(lambda n=n: s_qk_mm(n))
            for which in range(2):
                steps.append(lambda n=n, w=which: s_rope_ev(n, w))
                steps.append(lambda n=n, w=which: s_rope_rot(n, w))
                steps.append(lambda n=n, w=which: s_rope_tt(n, w))

        self._stats_steps(steps, pp, out, pr, qf, kf)
        return steps

    def attn_a(self, ev_ln):
        p, d = self.p, self.d
        self.push_scope()
        self.alloc_attn_common()
        self.obank_war = [ev_ln[-1]]
        self.dance_eng = "vector"
        vdil = self.ssb("vdil", [128, 16 * 130], BF16)
        odg = self.ssb("odg", [128, 2048], BF16)
        odt = self.ssb("odt", [128, 2048], BF16)
        ev_dil_war = None
        self.ev_odg_war = list(self.entry)
        self.ev_odt_war = list(self.entry)
        ev_res = []

        self._bg = self._prep_steps_a(0, ev_ln[-1])
        for pr in range(NP):
            pi = pr % 2
            self._bg_drain()
            if pr + 1 < NP:
                self._bg = self._prep_steps_a(pr + 1, ev_ln[-1])
                self._bg_skip = 10
            po = self.prep_out[pr]
            qf, kf, vpair = self.qf[pi], self.kf[pi], self.vpair[pi]

            st = self._sa_new_state([po["nega"]] + self.attn_prev0,
                                    [po["nega"]] + self.attn_prev1, pi,
                                    self.prev_o)
            ev_glb = []
            first_sco = [po["erow"]]
            dance_prev = {"ev": list(self.obank_war)}

            def score_fn(jj, psb, waits, fs=first_sco):
                c, jt = jj // 16, jj % 16
                esA = p.op("tensor", "matmul", self.ps(psb, 512),
                           kf[0:64, jt * 128:(jt + 1) * 128],
                           qf[0:64, c * 512:c * 512 + 512],
                           start=True, stop=True, tile_position=(0, 0),
                           wait=waits + fs, inc=False)
                esB = p.op("tensor", "matmul", self.ps(psb + 512, 512),
                           kf[64:128, jt * 128:(jt + 1) * 128],
                           qf[64:128, c * 512:c * 512 + 512],
                           start=True, stop=True, tile_position=(64, 0),
                           inc=True)
                if fs:
                    fs.clear()
                return esB

            def o_fn(jj, par, gate, extra, first, last):
                jt = jj % 16
                w = [gate] + extra + (dance_prev["ev"] if jt == 0 else [])
                evo = None
                for hl, po_ in ((0, 2048), (1, 2560)):
                    evo = p.op("tensor", "matmul", self.ps(po_, 512)[0:65, :],
                               vpair[:, jt * 130 + 65 * hl: jt * 130 + 65 * hl + 65],
                               self.pt[par][:, 512 * hl: 512 * hl + 512],
                               start=(jt == 0), stop=(jt == 15),
                               wait=w, inc=(hl == 1))
                return evo

            def post_o(oidx, jj, evo):
                if jj % 16 != 15:
                    return
                c = jj // 16
                etA, etB = self.o_dance(512, evo, evo)
                es = self._evac_of32(self.of32, 512 * c, 4, etA, etB,
                                     self.ev_of32_war)
                ev_glb.extend(es)
                dance_prev["ev"] = es

            self._sa_pipeline(st, list(range(64)), score_fn, o_fn,
                              False, None, None, [po["vdone"]], post_o=post_o)

            ev_b = p.op("sync", "dma_start", d["vp_d"].ap(), vpair[:],
                        wait=[po["vdone"], ev_dil_war]
                             + (list(self.entry) if pr == 0 else []),
                        dsem="dil", inc=True)
            ev_gather = None
            for ph in range(4):
                srcap = ap(d["vp_d"], [[8320, 32], [2080, 4], [520, 4], [1, 130]],
                           off=130 * ph)
                dstap = ap(vdil, [[2080, 32], [520, 4], [130, 4], [1, 130]],
                           off=ph * 32 * 2080)
                ev_gather = p.op("sync", "dma_start", dstap, srcap,
                                 wait=[ev_b], dsem="dil", inc=True)
            ev_dil = []

            def score_fn_d(jj, psb, waits):
                g, j4 = jj // 4, jj % 4
                esA = p.op("tensor", "matmul", self.ps(psb, 512),
                           kf[0:64, 512 * j4 + g: 512 * (j4 + 1): 4],
                           qf[0:64, g: 2048: 4],
                           start=True, stop=True, tile_position=(0, 0),
                           wait=waits, inc=False)
                esB = p.op("tensor", "matmul", self.ps(psb + 512, 512),
                           kf[64:128, 512 * j4 + g: 512 * (j4 + 1): 4],
                           qf[64:128, g: 2048: 4],
                           start=True, stop=True, tile_position=(64, 0),
                           inc=True)
                return esB

            def o_fn_d(jj, par, gate, extra, first, last):
                g, j4 = jj // 4, jj % 4
                w = [gate, ev_gather] + extra \
                    + (dance_prev["ev"] if j4 == 0 else [])
                evo = None
                for hl, po_ in ((0, 2048), (1, 2560)):
                    evo = p.op("tensor", "matmul", self.ps(po_, 512)[0:65, :],
                               vdil[:, (4 * g + j4) * 130 + 65 * hl:
                                    (4 * g + j4) * 130 + 65 * hl + 65],
                               self.pt[par][:, 512 * hl: 512 * hl + 512],
                               start=(j4 == 0), stop=(j4 == 3),
                               wait=w, inc=(hl == 1))
                return evo

            def post_o_d(oidx, jj, evo):
                if jj % 4 != 3:
                    return
                g = jj // 4
                etA, etB = self.o_dance(512, evo, evo)
                es = self._evac_of32(odg, 512 * g, 4, etA, etB,
                                     [ev_dil[-1]] if ev_dil
                                     else self.ev_odg_war)
                ev_dil.extend(es)
                dance_prev["ev"] = es

            ev_last_do = self._sa_pipeline(st, list(range(16)), score_fn_d,
                                           o_fn_d, False, None, None, [],
                                           post_o=post_o_d)
            ev_dil_war = ev_last_do
            ev_s1 = None
            for ph in range(4):
                srcap = ap(odg, [[2048, 32], [512, 4], [128, 4], [1, 128]],
                           off=32 * ph * 2048)
                dstap = ap(d["od_d"], [[4 * 2048, 32], [2048, 4], [512, 4], [1, 128]],
                           off=128 * ph)
                ev_s1 = p.op("sync", "dma_start", dstap, srcap,
                             wait=[ev_dil[-1]], dsem="odil", inc=True)
            ev_scat = p.op("sync", "dma_start", odt[:], d["od_d"].ap(),
                           wait=[ev_s1] + self.ev_odt_war, dsem="odil", inc=True)
            ev_dadd = p.op("vector", "tensor_tensor", self.of32[:], self.of32[:],
                           odt[:], ALU.add, wait=[ev_scat, ev_glb[-1]], inc=True)
            self.ev_odt_war = [ev_dadd]

            loc = {"add": [ev_dadd], "scat": [ev_s1]}

            def score_fn_l(wg, psb, waits):
                esA = esB = None
                for wi in range(4):
                    w = 4 * wg + wi
                    esA = p.op("tensor", "matmul",
                               self.ps(psb + wi * 128, 128),
                               kf[0:64, w * 128:(w + 1) * 128],
                               qf[0:64, w * 128:(w + 1) * 128],
                               start=True, stop=True, tile_position=(0, 0),
                               wait=waits, inc=False)
                    esB = p.op("tensor", "matmul",
                               self.ps(psb + 512 + wi * 128, 128),
                               kf[64:128, w * 128:(w + 1) * 128],
                               qf[64:128, w * 128:(w + 1) * 128],
                               start=True, stop=True, tile_position=(64, 0),
                               inc=(wi == 3))
                return esB

            def o_fn_l(wg, par, gate, extra, first, last):
                w0 = [gate] + extra + dance_prev["ev"]
                evo = None
                for hl, po_ in ((0, 2048), (1, 2560)):
                    for wi in range(4):
                        w = 4 * wg + wi
                        evo = p.op("tensor", "matmul",
                                   self.ps(po_ + wi * 128, 128)[0:65, :],
                                   vpair[:, w * 130 + 65 * hl:
                                         w * 130 + 65 * hl + 65],
                                   self.pt[par][:, 512 * hl + wi * 128:
                                              512 * hl + wi * 128 + 128],
                                   start=(wi == 0), stop=(wi == 3),
                                   wait=w0, inc=(hl == 1 and wi == 3))
                return evo

            def post_o_l(oidx, wg, evo):
                etA, etB = self.o_dance(512, evo, evo)
                e1, e2 = self._evac_of32(odg, 512 * wg, 4, etA, etB,
                                         loc["scat"])
                loc["add"] = [p.op("vector", "tensor_tensor",
                                   ap(self.of32, [[2048, 128], [1, 512]],
                                      off=wg * 512),
                                   ap(self.of32, [[2048, 128], [1, 512]],
                                      off=wg * 512),
                                   ap(odg, [[2048, 128], [1, 512]],
                                      off=wg * 512),
                                   ALU.add, wait=[e1, e2] + loc["add"],
                                   inc=True)]
                dance_prev["ev"] = [e1, e2]

            self._sa_pipeline(st, list(range(4)), score_fn_l, o_fn_l,
                              False, None, None, [], post_o=post_o_l)
            ev_loc_add = loc["add"][0]
            self.ev_odg_war = [ev_loc_add]
            self.obank_war = [ev_loc_add]

            ev_evac2 = self._pair_tail(pr, st, qf, kf, [ev_loc_add])
        self.wo_phase(d["w_out"], None, ev_evac2, self.wbl, ev_res)
        self.pop_scope()
        return ev_res

    def _pair_tail(self, pr, st, qf, kf, ev_done, extra_prev=()):
        p = self.p
        pi = pr % 2
        ecast0 = p.op("vector", "tensor_scalar", kf[:, 0:1024],
                      self.of32[:, 0:1024], 1.0, None, ALU.mult,
                      wait=ev_done + self.ev_cast_war + st["o_by_par"],
                      inc=True)
        ecast1 = p.op("vector", "tensor_scalar", qf[:, 0:1024],
                      self.of32[:, 1024:2048], 1.0, None, ALU.mult,
                      wait=[ecast0], inc=True)
        srcs = (kf, qf)
        casts = (ecast0, ecast1)
        war_tr = list(st["exp"][0]) + list(st["exp"][1]) \
            + st.get("extra_tr_war", [])
        ev_evac2 = None
        for cc in range(2):
            etr2 = None
            for tt in range(8):
                etr2 = p.op("tensor", "transpose",
                            self.ps(1024 + cc * 512 + tt * 64, 64).bitcast(BF16),
                            srcs[cc][:, tt * 128:(tt + 1) * 128],
                            self.ident_sb[:],
                            wait=[casts[cc]] + war_tr,
                            inc=(tt == 7))
            war_tr = []
            ev_evac2 = p.op("scalar", "activation",
                            self.ofm[:, pr * T + cc * 1024:
                                     pr * T + cc * 1024 + 1024],
                            self.ps(1024 + cc * 512, 512).bitcast(BF16),
                            AF.Copy, wait=[etr2], inc=True)
        self.ev_cast_war = []
        self.ev_of32_war = [ecast1]
        self.pair_done[pi] = [ev_evac2]
        lexp = list(st["exp"][1]) + list(extra_prev)
        self.attn_prev0 = lexp
        self.attn_prev1 = lexp + [ev_evac2]
        self.prev_o = list(st["o_by_par"])
        return ev_evac2

    def attn_b(self, ev_ln):
        p, d = self.p, self.d
        self.push_scope()
        self.alloc_attn_common()
        self.obank_war = [ev_ln[-1]]
        self.dance_eng = "scalar"
        latent = self.ssb("latent", [128, 2 * T], BF16)
        qtmp = self.xt32[:, 0:256]
        rt0 = self.xt32[:, 256:512]
        ropec = self.xt32[:, 512:768]
        ropes = self.xt32[:, 768:1024]
        ftw = self.ssb("ftw", [128, 3968], BF16)
        sig = self.ssb("sig", [128, 32], F32)
        mvrow_bf = self.ssb("mvrow_bf", [1, 130], BF16)
        mkb = self.ssb("mkb", [128, 1], BF16)
        self.ev_ftw = p.op("gpsimd", "dma_start", ftw[:], d["ftp"].ap(),
                           wait=[ev_ln[-1]], dsem="ftw", inc=True)
        ev_res = []

        ev_wd = self.stream_block(d["wkvd"], 0, 8, 256, self.wb[0], "gpsimd",
                                  "glat", wait=list(self.entry) + self.wbl[0])
        ev_le = None
        for m in range(2):
            emm = None
            for k in range(8):
                lhs = self.wb[0][:, k * 256 + m * 128: k * 256 + m * 128 + 128]
                for n in range(4):
                    emm = p.op("tensor", "matmul", self.ps(m * 2048 + n * 512, 512),
                               lhs,
                               self.xh_fm[:, k * T + n * 512: k * T + n * 512 + 512],
                               start=(k == 0), stop=(k == 7),
                               wait=[ev_wd, ev_ln[-1]], inc=(k == 7 and n == 3))
            ev_le = p.op("scalar", "activation", latent[:, m * T:(m + 1) * T],
                         self.ps(m * 2048, 2048), AF.Identity,
                         bias=self.bkvd_sb[:, m:m + 1],
                         wait=[emm, self.ev_setup["bkvd"]], inc=True)
        self.wbl[0] = [emm]
        self.prep_last = [ev_le]
        self.attn_prev0 = [ev_le]
        self.attn_prev1 = [ev_le]
        self.obank_war = [ev_le]

        self._bg = self._prep_steps_b(0, ev_ln[-1], latent, qtmp, ropec,
                                      ropes, rt0, ev_le)
        for pr in range(NP):
            pi = pr % 2
            self._bg_drain()
            if pr + 1 < NP:
                self._bg = self._prep_steps_b(pr + 1, ev_ln[-1], latent, qtmp,
                                              ropec, ropes, rt0, ev_le)
                self._bg_skip = 10
            po = self.prep_out[pr]
            qf, kf, vpair = self.qf[pi], self.kf[pi], self.vpair[pi]

            st = self._sa_new_state([po["nega"]] + self.attn_prev0,
                                    [po["nega"]] + self.attn_prev1, pi,
                                    self.prev_o)
            ev_glb = []
            first_sco = [po["erow"]]
            dance_prev = {"ev": list(self.obank_war)}

            def score_fn(jj, psb, waits, fs=first_sco):
                c, jt = jj // 16, jj % 16
                esA = p.op("tensor", "matmul", self.ps(psb, 512),
                           kf[0:64, jt * 128:(jt + 1) * 128],
                           qf[0:64, c * 512:c * 512 + 512],
                           start=False, stop=True, tile_position=(0, 0),
                           wait=waits + fs, inc=False)
                esB = p.op("tensor", "matmul", self.ps(psb + 512, 512),
                           kf[64:128, jt * 128:(jt + 1) * 128],
                           qf[64:128, c * 512:c * 512 + 512],
                           start=False, stop=True, tile_position=(64, 0),
                           inc=True)
                if fs:
                    fs.clear()
                return esB

            def wt_tile_fn(jj):
                c, jt = jj // 16, jj % 16
                off = 1920 + 512 * c - 128 * jt
                return ftw[:, off:off + 512]

            def o_fn(jj, par, gate, extra, first, last):
                jt = jj % 16
                w = [gate] + extra + (dance_prev["ev"] if jt == 0 else [])
                evo = None
                for hl, po_ in ((0, 2048), (1, 2560)):
                    evo = p.op("tensor", "matmul", self.ps(po_, 512)[0:65, :],
                               vpair[:, jt * 130 + 65 * hl: jt * 130 + 65 * hl + 65],
                               self.pt[par][:, 512 * hl: 512 * hl + 512],
                               start=(jt == 0), stop=(jt == 15),
                               wait=w, inc=(hl == 1))
                return evo

            def post_o(oidx, jj, evo):
                if jj % 16 != 15:
                    return
                c = jj // 16
                etA, etB = self.o_dance(512, evo, evo)
                es = self._evac_of32(self.of32, 512 * c, 4, etA, etB,
                                     self.ev_of32_war)
                ev_glb.extend(es)
                dance_prev["ev"] = es

            self._sa_pipeline(st, list(range(64)), score_fn, o_fn,
                              True, None, wt_tile_fn, [po["vdone"]],
                              post_o=post_o)

            emk2 = p.op("vector", "reduce_sum", self.rz[:, 4:5], kf[:],
                        axis=AX.X, wait=[ev_glb[-1]], inc=True)
            emk3 = p.op("vector", "tensor_scalar", self.rz[:, 5:6],
                        self.rz[:, 4:5], 1.0 / T, None, ALU.mult,
                        wait=[emk2], inc=True)
            emk4 = p.op("vector", "tensor_copy", mkb[:], self.rz[:, 5:6],
                        wait=[emk3], inc=True)
            ems = None
            for hl in range(2):
                b0 = 64 * hl
                for it in range(NT):
                    ems = p.op("tensor", "matmul",
                               self.ps(16 * hl + it, 1)[0:128, 0:1],
                               qf[b0:b0 + 64, it * 128:(it + 1) * 128],
                               mkb[b0:b0 + 64, 0:1],
                               start=(hl == 0 and it == 0),
                               stop=(hl == 1 and it == NT - 1),
                               tile_position=(b0, 0),
                               wait=[emk4] + st["exp"][1],
                               inc=(hl == 1 and it == NT - 1))
            esig = p.op("scalar", "activation", sig[:], self.ps(0, 32),
                        AF.Exp, scale=-8.0, wait=[ems], inc=True)
            esig2 = p.op("vector", "tensor_scalar", sig[:], sig[:], 1.0, None,
                         ALU.add, wait=[esig], inc=True)
            esig3 = p.op("vector", "reciprocal", sig[:], sig[:],
                         wait=[esig2], inc=True)
            esig4 = p.op("vector", "tensor_scalar", sig[:], sig[:], 0.2, None,
                         ALU.mult, wait=[esig3], inc=True)
            emv = emve = None
            for hl in range(2):
                for jt in range(NT):
                    emv = p.op("tensor", "matmul",
                               self.ps(512 * (1 + hl), 65)[0:1, 0:65],
                               self.ones_sb[:, 0:1],
                               vpair[:, jt * 130 + 65 * hl: jt * 130 + 65 * hl + 65],
                               start=(jt == 0), stop=(jt == NT - 1),
                               wait=[po["vdone"]] + st["exp"][1],
                               inc=(jt == NT - 1))
                emve = p.op("scalar", "activation",
                            mvrow_bf[0:1, 65 * hl:65 * hl + 65],
                            self.ps(512 * (1 + hl), 65)[0:1, 0:65], AF.Copy,
                            scale=1.0 / T, wait=[emv], inc=True)
            ebc2 = None
            for hl in range(2):
                ebc2 = p.op("tensor", "matmul",
                            self.ps(1536 + 64 * hl, 64)[0:128, 0:64],
                            self.ones_sb[0:1, :],
                            mvrow_bf[:, 65 * hl: 65 * hl + 64],
                            start=(hl == 0), stop=(hl == 1),
                            wait=[emve, ems] + st["exp"][1], inc=(hl == 1))
            ein = None
            for hl in range(2):
                e1 = p.op("vector", "tensor_tensor",
                          ap(kf, [[2048, 128], [64, 16], [1, 64]], off=1024),
                          ap(self.PS, [[4096, 128], [0, 16], [1, 64]],
                             off=1536 + 64 * hl),
                          ap(sig, [[32, 128], [1, 16], [0, 64]], off=16 * hl),
                          ALU.mult,
                          wait=[ebc2, esig4, st["o_by_par"][0], st["o_by_par"][1],
                                ein], inc=True)
                ein = p.op("vector", "tensor_tensor",
                           ap(self.of32, [[2048, 128], [128, 16], [1, 64]],
                              off=64 * hl),
                           ap(self.of32, [[2048, 128], [128, 16], [1, 64]],
                              off=64 * hl),
                           ap(kf, [[2048, 128], [64, 16], [1, 64]], off=1024),
                           ALU.add, wait=[e1], inc=True)

            st["extra_tr_war"] = [esig]
            self.obank_war = [ein]
            ev_evac2 = self._pair_tail(pr, st, qf, kf, [ein], extra_prev=[emve])
        self.wo_phase(d["wo_b"], self.bo_bf, ev_evac2, self.wbl, ev_res)
        self.pop_scope()
        return ev_res

    def wo_phase(self, wdram, bias_bf, ev_pair_done, wbl, ev_res):
        p = self.p
        self.stream_block(wdram, 0, 6, 512, self.wb[0], "gpsimd", "wo",
                          wait=wbl[0] + wbl[1] + [ev_pair_done])
        self.stream_block(wdram, 6 * 128, 6, 512, self.wb[1], "gpsimd", "wo")
        self.stream_block(wdram, 12 * 128, 2, 512, self.pt[0], "gpsimd", "wo")
        ev_w = self.stream_block(wdram, 14 * 128, 2, 512, self.pt[1], "gpsimd", "wo")

        def blk(b):
            if b < 6:
                return self.wb[0][:, b * 512:(b + 1) * 512]
            if b < 12:
                return self.wb[1][:, (b - 6) * 512:(b - 5) * 512]
            if b < 14:
                return self.pt[0][:, (b - 12) * 512:(b - 11) * 512]
            return self.pt[1][:, (b - 14) * 512:(b - 13) * 512]

        emm_prev = [ev_pair_done, ev_pair_done]
        for t in range(NT):
            pp0 = 2048 * (t % 2)
            eb = None
            for k in range(8):
                lhs = self.ofm[:, k * T + t * 128: k * T + t * 128 + 128]
                for n in range(2):
                    last = (bias_bf is None and k == 7)
                    eb = p.op("tensor", "matmul", self.ps(pp0 + n * 512, 512),
                              lhs, blk(k * 2 + n), start=(k == 0), stop=last,
                              wait=[ev_w, ev_pair_done, emm_prev[t % 2]],
                              inc=(last and n == 1))
            if bias_bf is not None:
                for n in range(2):
                    eb = p.op("tensor", "matmul", self.ps(pp0 + n * 512, 512),
                              self.ones_sb[0:1, :],
                              bias_bf[:, n * 512: n * 512 + 512],
                              start=False, stop=True,
                              wait=[self.ev_setup["bo"], self.ev_setup["ones"]],
                              inc=(n == 1))
            xt = self.x_sb[:, t * D:(t + 1) * D]
            er = p.op("vector", "tensor_tensor", xt, xt, self.ps(pp0, 1024),
                      ALU.add, wait=[eb], inc=True)
            self.ln_stats(t, er)
            emm_prev[t % 2] = er
            ev_res.append(er)
        for s2 in range(2):
            self.prefetch_ffn[f"w{s2}"] = self.stream_block(
                self.d["w1r"], s2 * D, 8, 128, self.wb[0], "gpsimd",
                f"pw{s2}", col0=s2 * 1024, wait=[eb])
        return ev_res

    def build(self):
        self.setup()
        ev_store = ()
        for s in range(self.nseq):
            evl = self.load_x(s, extra_wait=ev_store)
            for t in range(NT):
                self.ln_stats(t, evl[t % 4])
            ev_x = [evl[-1]] * NT
            for i, sl in enumerate(self.sublayers):
                if i + 1 < len(self.sublayers):
                    nxt = self.sublayers[i + 1]
                elif s + 1 < self.nseq:
                    nxt = self.sublayers[0]
                else:
                    nxt = None
                self.entry = self.p.snapshot()
                ev_ln = self.layernorm(ev_x)
                if sl == "ffn":
                    ev_x = self.ffn(ev_ln, nxt=nxt,
                                    stats_ok=(i + 1 < len(self.sublayers)))
                elif sl == "attn_a":
                    ev_x = self.attn_a(ev_ln)
                elif sl == "attn_b":
                    ev_x = self.attn_b(ev_ln)
            ev_store = self.store_x(s, ev_x)
        self.p.op("tensor", None, wait=ev_store)
        self.p.barrier()
        self.p.emit()
        self.es.close()
        return self.nc


N_CORES = 8


def kernel(**inputs):
    m = MK(sublayers=("attn_a", "ffn", "attn_b", "ffn"), nseq=2)
    nc = m.build()
    in_maps = [host_inputs(inputs, core) for core in range(N_CORES)]
    from concourse.bass_utils import run_bass_kernel_spmd
    res = run_bass_kernel_spmd(nc, in_maps, core_ids=list(range(N_CORES)))
    out = np.concatenate([res.results[c]["x_out"] for c in range(N_CORES)], axis=0)
    return out.astype(np.float32, copy=False)


# version 16
# speedup vs baseline: 1.0383x; 1.0383x over previous
import numpy as np
import ml_dtypes
from contextlib import ExitStack

import concourse.bass as bass
import concourse.mybir as mybir

F32 = mybir.dt.float32
BF16 = mybir.dt.bfloat16
AF = mybir.ActivationFunctionType
ALU = mybir.AluOpType
AX = mybir.AxisListType
BF = ml_dtypes.bfloat16

ENGINES = ("tensor", "scalar", "vector", "gpsimd", "sync")
NT = 16
T = 2048
D = 1024
NP = 8
DH = 64
DFF = 4096


class Prog:
    def __init__(self, nc):
        self.nc = nc
        self.plan = {e: [] for e in ENGINES}
        self.count = {}
        self.lastwait = {e: {} for e in ENGINES}
        self.dma_sems = set()
        self.sems = {}

    def op(self, eng, meth, *args, wait=(), inc=False, dsem=None, **kw):
        is_dma = meth in ("dma_start", "dma_start_transpose")
        sname = (dsem or (eng + "_d")) if is_dma else eng
        if is_dma:
            self.dma_sems.add(sname)
        waits = []
        for ev in wait:
            if ev is None:
                continue
            p, v = ev
            if v > self.lastwait[eng].get(p, 0):
                if p in self.dma_sems and v < self.count.get(p, 0):
                    raise RuntimeError(
                        f"stale DMA event: {eng} waiting ({p},{v}) but {p} already "
                        f"at {self.count[p]} — only latest event per sem is safe")
                if p == sname and v > self.count.get(sname, 0):
                    raise RuntimeError(f"{eng} waits own future {v} on {p}")
                waits.append((p, v))
                self.lastwait[eng][p] = v
        ev_out = None
        amt = 0
        if inc:
            amt = 16 if is_dma else 1
            self.count[sname] = self.count.get(sname, 0) + amt
            ev_out = (sname, self.count[sname])
        if waits or meth is not None:
            self.plan[eng].append((waits, meth, args, kw, amt, sname))
        return ev_out

    def snapshot(self):
        return [(s, v) for s, v in self.count.items() if v > 0 and s in ENGINES]

    def barrier(self, engines=ENGINES):
        evs = [self.op(e, "drain", fusable=False, inc=True) for e in engines]
        for e in engines:
            self.op(e, None, wait=evs)

    def emit(self):
        nc = self.nc
        names = set(self.count) | {w[0] for e in ENGINES
                                   for (ws, *_r) in self.plan[e] for w in ws}
        with ExitStack() as es:
            for s in sorted(names):
                self.sems[s] = es.enter_context(nc.semaphore("sem_" + s))
            with nc.Block() as block:
                for e in ENGINES:
                    getattr(block, e)(lambda h, e=e: self._run(e, h))

    def _run(self, eng, h):
        sems = self.sems
        for waits, meth, args, kw, amt, sname in self.plan[eng]:
            for (p, v) in waits:
                h.wait_ge(sems[p], v)
            if meth is None:
                continue
            inst = getattr(h, meth)(*args, **kw)
            if amt:
                inst.then_inc(sems[sname], amt)


def ap(t, dims, off=0):
    return bass.AP(t, off, [list(d) for d in dims])


def host_inputs(inputs, core, nseq=2):
    f = np.float32
    d = {}
    x = np.asarray(inputs["L0_attn_a_x"], dtype=f)
    d["x_in"] = np.ascontiguousarray(x[2 * core:2 * core + nseq])

    wqkv = np.asarray(inputs["L0_attn_a_w_qkv"], dtype=f)
    pair_blocks = lambda w: np.ascontiguousarray(
        w.reshape(D, NP, 128).transpose(1, 0, 2).reshape(NP * D, 128))
    rhs_blocks = lambda w: np.ascontiguousarray(
        w.reshape(w.shape[0] // 128, 128, w.shape[1] // 512, 512)
        .transpose(0, 2, 1, 3).reshape(-1, 512))
    d["wq_a"] = pair_blocks(wqkv[:, 0:D])
    d["wk_a"] = pair_blocks(wqkv[:, D:2 * D])
    d["wv_a"] = pair_blocks(np.ascontiguousarray(wqkv[:, 2 * D:3 * D]))
    d["w_out"] = rhs_blocks(np.asarray(inputs["L0_attn_a_w_out"], dtype=f))

    w1 = np.asarray(inputs["L0_ffn_w1"], dtype=f)
    d["w1r"] = np.ascontiguousarray(
        w1.reshape(D, 32, 128).transpose(1, 0, 2).reshape(32 * D, 128))
    d["w2r"] = rhs_blocks(np.asarray(inputs["L0_ffn_w2"], dtype=f))
    b1 = np.asarray(inputs["L0_ffn_b1"], dtype=f)
    d["b1_t"] = np.ascontiguousarray(b1.reshape(32, 128).T)
    d["b2_r"] = np.ascontiguousarray(np.asarray(inputs["L0_ffn_b2"], dtype=f).reshape(1, D))

    d["wq_b"] = pair_blocks(np.asarray(inputs["L0_attn_b_Wq"], dtype=f))
    d["wkvd"] = np.ascontiguousarray(np.asarray(inputs["L0_attn_b_Wkv_down"], dtype=f))
    wkvu = np.asarray(inputs["L0_attn_b_Wkv_up"], dtype=f).reshape(256, 16, 2, DH)
    ku = wkvu[:, :, 0, :].reshape(256, NP, 128).transpose(1, 0, 2)
    vu = wkvu[:, :, 1, :].reshape(256, NP, 128).transpose(1, 0, 2)
    d["wkvu_k"] = np.ascontiguousarray(ku.reshape(NP * 256, 128))
    d["wkvu_v"] = np.ascontiguousarray(vu.reshape(NP * 256, 128))
    d["wo_b"] = rhs_blocks(np.asarray(inputs["L0_attn_b_Wo"], dtype=f))
    d["bq_t"] = np.ascontiguousarray(np.asarray(inputs["L0_attn_b_bq"], dtype=f).reshape(8, 128).T)
    d["bkvd_t"] = np.ascontiguousarray(
        np.asarray(inputs["L0_attn_b_bkv_down"], dtype=f).reshape(2, 128).T)
    bku = np.asarray(inputs["L0_attn_b_bkv_up"], dtype=f).reshape(16, 2, DH)
    d["bku_k_t"] = np.ascontiguousarray(bku[:, 0, :].reshape(8, 128).T)
    d["bku_v_r"] = np.ascontiguousarray(bku[:, 1, :].reshape(1, D))
    d["bo_r"] = np.ascontiguousarray(np.asarray(inputs["L0_attn_b_bo"], dtype=f).reshape(1, D))

    d["ident"] = np.eye(128, dtype=BF)
    d["ident32"] = np.eye(128, dtype=f)
    rp = np.zeros((128, 128), dtype=f)
    for m in range(128):
        rp[(m // 64) * 64 + (m + 32) % 64, m] = 1.0
    d["rotp"] = rp
    idx = np.arange(T, dtype=np.float64)
    delta = np.arange(-(T - 1), T, dtype=np.float64)
    ftvec = (-0.1 * np.log(np.abs(delta) + 1.0) + np.log(
        np.exp(-np.maximum(delta, 0.0) * 0.01) + 1e-9)).astype(f)
    d["ftp"] = np.ascontiguousarray(
        np.stack([ftvec[127 - j: 127 - j + 3968] for j in range(128)]))
    inv = 10000.0 ** (-np.arange(0, 32, dtype=np.float64) / 32.0)
    fr = np.outer(idx, inv)
    cos = np.cos(fr).T.astype(f)
    sin = np.sin(fr).T.astype(f)
    d["ropeC"] = np.ascontiguousarray(np.tile(cos, (4, 1)))
    d["ropeS"] = np.ascontiguousarray(np.concatenate([-sin, sin, -sin, sin], axis=0))
    return d


INPUT_SPECS = {
    "x_in": (None, F32),
    "wq_a": ([NP * D, 128], F32), "wk_a": ([NP * D, 128], F32),
    "wv_a": ([NP * D, 128], F32), "w_out": ([16 * 128, 512], F32),
    "w1r": ([32 * D, 128], F32), "b1_t": ([128, 32], F32),
    "w2r": ([64 * 128, 512], F32), "b2_r": ([1, D], F32),
    "wq_b": ([NP * D, 128], F32), "wkvd": ([D, 256], F32),
    "wkvu_k": ([NP * 256, 128], F32), "wkvu_v": ([NP * 256, 128], F32),
    "wo_b": ([16 * 128, 512], F32),
    "bq_t": ([128, 8], F32), "bkvd_t": ([128, 2], F32), "bku_k_t": ([128, 8], F32),
    "bku_v_r": ([1, D], F32), "bo_r": ([1, D], F32),
    "ident": ([128, 128], BF16), "ident32": ([128, 128], F32),
    "rotp": ([128, 128], F32), "ftp": ([128, 3968], F32),
    "ropeC": ([128, T], F32), "ropeS": ([128, T], F32),
}


class MK:
    def __init__(self, sublayers=("attn_a", "ffn", "attn_b", "ffn"), nseq=2):
        self.sublayers = sublayers
        self.nseq = nseq
        nc = self.nc = bass.Bass("TRN2", target_bir_lowering=False)
        self.p = Prog(nc)
        self.es = ExitStack()
        self.d = {}
        for name, (shape, dt) in INPUT_SPECS.items():
            if name == "x_in":
                shape = [nseq, T, D]
            self.d[name] = nc.dram_tensor(name, shape, dt, kind="ExternalInput")
        self.d["x_out"] = nc.dram_tensor("x_out", [nseq, T, D], F32, kind="ExternalOutput")
        if "attn_a" in sublayers:
            self.d["vp_d"] = nc.dram_tensor("vp_d", [128, 2080], BF16, kind="Internal")
            self.d["od_d"] = nc.dram_tensor("od_d", [128, 2048], BF16, kind="Internal")

        sb = lambda name, shape, dt: self.es.enter_context(nc.sbuf_tensor(name, shape, dt))
        self.x_sb = sb("x_sb", [128, NT * D], F32)
        self.xh_fm = sb("xh_fm", [128, 8 * T], BF16)
        self.wb = [sb(f"wb{i}", [128, 3072], BF16) for i in range(2)]
        self.ident_sb = sb("ident_sb", [128, 128], BF16)
        self.ident32_sb = sb("ident32_sb", [128, 128], F32)
        self.rotp_sb = sb("rotp_sb", [128, 128], F32)
        self.ones_sb = sb("ones_sb", [128, 128], BF16)
        self.xt32 = sb("xt32", [128, 1024], F32)
        self.eps_sb = sb("eps_sb", [128, 1], F32)
        self.ln_sum = sb("ln_sum", [128, NT], F32)
        self.ln_sq = sb("ln_sq", [128, NT], F32)
        self.ln_m = sb("ln_m", [128, NT], F32)
        self.ln_m2 = sb("ln_m2", [128, NT], F32)
        self.ln_var = sb("ln_var", [128, NT], F32)
        self.ln_rs = sb("ln_rs", [128, NT], F32)
        self.ln_b = sb("ln_b", [128, NT], F32)
        self.b1_sb = sb("b1_sb", [128, 32], F32)
        self.bq_sb = sb("bq_sb", [128, 8], F32)
        self.bq8_sb = sb("bq8_sb", [128, 8], F32)
        self.bkvd_sb = sb("bkvd_sb", [128, 2], F32)
        self.bkuk_sb = sb("bkuk_sb", [128, 8], F32)
        self.bkuv_bf = sb("bkuv_bf", [1, D], BF16)
        self.bo_bf = sb("bo_bf", [1, D], BF16)
        self.PS = self.es.enter_context(nc.psum_tensor("PS", [128, 4096], F32))
        self._scopes = []
        self.prefetch = {}
        self.prefetch_ffn = {}
        self.ln_pend = None
        self.ln_e8 = None
        self.ln_stat_prev = None

    def push_scope(self):
        self._scopes.append(ExitStack())

    def pop_scope(self):
        self._scopes.pop().close()

    def ssb(self, name, shape, dt):
        self._uid = getattr(self, "_uid", 0) + 1
        return self._scopes[-1].enter_context(
            self.nc.sbuf_tensor(f"{name}_{self._uid}", shape, dt))

    def alloc_attn_common(self):
        ent = self.p.snapshot()
        self.obuf = self.ssb("obuf", [65, 1024], F32)
        self.ev_obuf_war = list(ent)
        self.ev_cast_war = list(ent)
        self.ev_of32_war = list(ent)
        self.ofm = self.ssb("ofm", [128, 8 * T], BF16)
        self.qf = [self.ssb(f"qf{i}", [128, T], BF16) for i in range(2)]
        self.kf = [self.ssb(f"kf{i}", [128, T], BF16) for i in range(2)]
        self.vpair = [self.ssb(f"vp{i}", [128, 16 * 130], BF16) for i in range(2)]
        self.of32 = self.ssb("of32", [128, T], F32)
        self.pt = [self.ssb(f"pt{i}", [128, 1024], BF16) for i in range(2)]
        self.smax = self.ssb("smax", [128, 16], F32)
        self.smax2 = self.ssb("smax2", [128, 2], F32)
        self.sa1b = self.ssb("sa1b", [1, 2], BF16)
        self.nega = self.ssb("nega", [128, 2], F32)
        self.rz = self.ssb("rz", [128, 16], F32)
        self.pair_done = [list(ent), list(ent)]
        self.prep_last = list(ent)
        self.attn_prev0 = list(ent)
        self.attn_prev1 = list(ent)
        self.obank_war = list(ent)
        self.prev_o = [None, None]
        self.wbl = [list(ent), list(ent)]
        self.smax_war = [list(ent), list(ent)]
        self.smax2_war = [list(ent), list(ent)]
        self.sa1b_war = [list(ent), list(ent)]
        self.rope_last = list(ent)
        self.prep_out = {}
        self._bg = []
        self._bg_skip = 0

    def ps(self, c0, n):
        return self.PS[:, c0:c0 + n]

    def dma_in(self, dst_ap, dram, row0, nrows, ncols, eng="sync", wait=(), inc=True):
        src = ap(dram, [[ncols, nrows], [1, ncols]], off=row0 * ncols)
        return self.p.op(eng, "dma_start", dst_ap, src, wait=wait, inc=inc)

    def stream_block(self, dram, row0, nk, ncols, buf, eng, dsem, wait=(), col0=0):
        src = ap(dram, [[ncols, 128], [128 * ncols, nk], [1, ncols]], off=row0 * ncols)
        dst = ap(buf, [[buf.shape[1], 128], [ncols, nk], [1, ncols]], off=col0)
        return self.p.op(eng, "dma_start", dst, src, wait=wait, dsem=dsem, inc=True)

    def setup(self):
        p, d = self.p, self.d
        ev = {}
        for name, dst in (("ident", self.ident_sb), ("ident32", self.ident32_sb),
                          ("rotp", self.rotp_sb),
                          ("b1", self.b1_sb), ("bq", self.bq_sb),
                          ("bkvd", self.bkvd_sb), ("bkuk", self.bkuk_sb)):
            fence = p.op("sync", "dma_start", dst[:],
                         d[{"ident": "ident", "ident32": "ident32", "b1": "b1_t",
                            "rotp": "rotp",
                            "bq": "bq_t", "bkvd": "bkvd_t",
                            "bkuk": "bku_k_t"}[name]].ap(), dsem="setup", inc=True)
        for name, dst, srcn in (("bkuv", self.bkuv_bf, "bku_v_r"),
                                ("bo", self.bo_bf, "bo_r")):
            gfence = p.op("gpsimd", "dma_start", dst[:], d[srcn].ap(),
                          dsem="setupg", inc=True)
        for k in ("ident", "ident32", "rotp", "b1", "bq", "bkvd", "bkuk"):
            ev[k] = fence
        for k in ("bkuv", "bo"):
            ev[k] = gfence
        ev["ones"] = p.op("gpsimd", "memset", self.ones_sb[:], 1.0, inc=True)
        ev["bq8"] = p.op("vector", "tensor_scalar", self.bq8_sb[:], self.bq_sb[:],
                         0.125, None, ALU.mult, wait=[ev["bq"]], inc=True)
        ev["eps"] = p.op("gpsimd", "memset", self.eps_sb[:], 1e-5, inc=True)
        self.ev_prep_final = None
        self.ev_setup = ev

    def load_x(self, s, extra_wait=()):
        p, d = self.p, self.d
        x = d["x_in"]
        evs = [None] * 4
        for t in range(NT):
            evs[t % 4] = p.op("sync", "dma_start",
                              self.x_sb[:, t * D:(t + 1) * D],
                              ap(x, [[D, 128], [1, D]],
                                 off=(s * T + t * 128) * D),
                              wait=extra_wait, dsem=f"io{t % 4}", inc=True)
        return evs

    def store_x(self, s, ev_ready):
        p, d = self.p, self.d
        x = d["x_out"]
        for t in range(NT):
            dst = ap(x, [[D, 128], [1, D]], off=(s * T + t * 128) * D)
            e = p.op("sync", "dma_start", dst, self.x_sb[:, t * D:(t + 1) * D],
                     wait=[ev_ready[t]], dsem="io", inc=True)
        return [e]

    def ln_stats(self, t, ev):
        p = self.p
        xt = self.x_sb[:, t * D:(t + 1) * D]
        e1 = p.op("scalar", "activation", self.xt32[:, 0:D], xt, AF.Copy,
                  accum_out=self.ln_sum[:, t:t + 1],
                  wait=[ev, self.ln_e8, self.ln_stat_prev], inc=True)
        e2 = p.op("scalar", "activation", self.xt32[:, 0:D], xt, AF.Square,
                  accum_out=self.ln_sq[:, t:t + 1], wait=[e1], inc=True)
        self.ln_stat_prev = e2
        self.ln_pend = [e2]
        return e2

    def layernorm(self, ev_x):
        p = self.p
        if self.ln_pend is not None:
            ev_stats = self.ln_pend
            self.ln_pend = None
        else:
            ev_stats = []
            for t in range(NT):
                xt = self.x_sb[:, t * D:(t + 1) * D]
                e1 = p.op("vector", "reduce_sum", self.ln_sum[:, t:t + 1], xt,
                          axis=AX.X, wait=[ev_x[t], self.ln_e8], inc=True)
                e2 = p.op("vector", "scalar_tensor_tensor", self.xt32[:, 0:D],
                          xt, 1.0, xt,
                          ALU.mult, ALU.mult, accum_out=self.ln_sq[:, t:t + 1],
                          wait=[e1], inc=True)
                ev_stats.append(e2)
        e3 = p.op("vector", "tensor_scalar", self.ln_m[:], self.ln_sum[:], 1.0 / D, None,
                  ALU.mult, wait=ev_stats, inc=True)
        e4 = p.op("vector", "tensor_tensor", self.ln_m2[:], self.ln_m[:], self.ln_m[:],
                  ALU.mult, wait=[e3], inc=True)
        e5 = p.op("vector", "scalar_tensor_tensor", self.ln_var[:], self.ln_sq[:], 1.0 / D,
                  self.ln_m2[:], ALU.mult, ALU.subtract, wait=[e4], inc=True)
        e6 = p.op("scalar", "activation", self.ln_var[:], self.ln_var[:], AF.Ln,
                  bias=self.eps_sb[:], wait=[e5, self.ev_setup["eps"]], inc=True)
        e7 = p.op("scalar", "activation", self.ln_rs[:], self.ln_var[:], AF.Exp,
                  scale=-0.5, wait=[e6], inc=True)
        e8 = p.op("vector", "scalar_tensor_tensor", self.ln_b[:], self.ln_m[:], -1.0,
                  self.ln_rs[:], ALU.mult, ALU.mult, wait=[e7], inc=True)
        self.ln_e8 = e8
        ev_out = []
        ev_evac_par = [None, None]
        etr_par = [None, None]
        self.push_scope()
        xbuf = [self.xt32, self.ssb("ln_x2", [128, 1024], F32)]
        for t in range(NT):
            xt = self.x_sb[:, t * D:(t + 1) * D]
            xb = xbuf[t % 2]
            en = p.op("vector", "tensor_scalar", xb[:, 0:D], xt,
                      self.ln_rs[:, t:t + 1], self.ln_b[:, t:t + 1], ALU.mult, ALU.add,
                      wait=[e8, etr_par[t % 2]], inc=True)
            po = 2048 * (t % 2)
            etr = None
            for f in range(8):
                etr = p.op("tensor", "transpose",
                           self.ps(po, 1024)[:, f * 128:(f + 1) * 128],
                           xb[:, f * 128:(f + 1) * 128], self.ident32_sb[:],
                           wait=[en, ev_evac_par[t % 2], self.ev_setup["ident32"]],
                           inc=(f == 7))
            etr_par[t % 2] = etr
            dst = ap(self.xh_fm, [[8 * T, 128], [T, 8], [1, 128]], off=t * 128)
            ev_evac_par[t % 2] = p.op("scalar", "activation", dst,
                                      self.ps(po, 1024)[:], AF.Copy,
                                      wait=[etr], inc=True)
            ev_out.append(ev_evac_par[t % 2])
        self.pop_scope()
        return ev_out

    def _emit_prefetch(self, nxt):
        p, d = self.p, self.d
        self.prefetch = {}
        if nxt == "attn_a":
            ev = self.stream_block(d["wv_a"], 0, 8, 128, self.wb[0], "gpsimd",
                                   "pfv", wait=list(self.entry), col0=2048)
            self.prefetch = {"sl": "attn_a", "wv": ev}
        elif nxt == "attn_b":
            ek = self.stream_block(d["wkvu_k"], 0, 2, 128, self.wb[0],
                                   "gpsimd", "pfk", wait=list(self.entry),
                                   col0=2048)
            ev = self.stream_block(d["wkvu_v"], 0, 2, 128, self.wb[0],
                                   "gpsimd", "pfv", col0=2304)
            self.prefetch = {"sl": "attn_b", "wk": ek, "wv": ev}

    def ffn(self, ev_ln, nxt=None, stats_ok=False):
        p, d = self.p, self.d
        self.push_scope()
        hbuf = self.ssb("hbuf", [128, 32 * 1024], BF16)
        b2_bf = self.ssb("b2_bf", [1, D], BF16)
        ev_b2 = p.op("gpsimd", "dma_start", b2_bf[:], d["b2_r"].ap(),
                     wait=list(self.entry), dsem="setupg", inc=True)
        w1 = d["w1r"]
        w2 = d["w2r"]
        ev_res = [None] * NT
        ev_res_last = None
        wsl = [None] * 4
        first = list(self.entry)
        for half in range(2):
            tok0 = half * 1024
            ev_h = [None] * 32
            for m in range(32):
                if half == 0 and m == 4:
                    self._emit_prefetch(nxt)
                s = m % 4
                bi, c0 = s // 2, (s % 2) * 1024
                if half == 0 and m < 2 and self.prefetch_ffn:
                    ev_w = self.prefetch_ffn[f"w{m}"]
                else:
                    ev_w = self.stream_block(w1, m * D, 8, 128, self.wb[bi],
                                             "gpsimd", f"wsl{s}", col0=c0,
                                             wait=[wsl[s]]
                                             + (first if half == 0 and m < 4
                                                else []))
                pp = self.ps(2048 * (m % 2), 1024)
                emm = None
                for k in range(8):
                    lhs = self.wb[bi][:, c0 + k * 128: c0 + k * 128 + 128]
                    for n in range(2):
                        rhs = self.xh_fm[:, k * T + tok0 + n * 512:
                                         k * T + tok0 + n * 512 + 512]
                        emm = p.op("tensor", "matmul", pp[:, n * 512:n * 512 + 512],
                                   lhs, rhs, start=(k == 0), stop=(k == 7),
                                   wait=[ev_w, ev_ln[-1],
                                         ev_h[m - 2] if m >= 2 else None,
                                         ev_res_last],
                                   inc=(k == 7 and n == 1))
                wsl[s] = emm
                ev_h[m] = p.op("scalar", "activation", hbuf[:, m * 1024:(m + 1) * 1024],
                               pp[:], AF.Gelu, bias=self.b1_sb[:, m:m + 1],
                               wait=[emm, self.ev_setup["b1"]]
                               + (self.entry if (half == 0 and m < 2) else []),
                               inc=True)
            ev_sweep_res = []
            for n in range(2):
                emm_t = [None] * 8
                for c in range(32):
                    s = c % 4
                    bi, c0 = s // 2, (s % 2) * 1024
                    ev_w2 = self.stream_block(w2, (c * 2 + n) * 128, 1, 512,
                                              self.wb[bi], "gpsimd", f"wsl{s}",
                                              col0=c0, wait=[wsl[s]])
                    for t8 in range(8):
                        lhs = hbuf[:, c * 1024 + t8 * 128: c * 1024 + t8 * 128 + 128]
                        pp = self.ps(512 * t8, 512)
                        w0 = [ev_w2, ev_h[c]]
                        if c == 0:
                            w0 += [ev_h[31]] + ev_sweep_res
                        emm_t[t8] = p.op(
                            "tensor", "matmul", pp, lhs,
                            self.wb[bi][:, c0:c0 + 512],
                            start=(c == 0), stop=False, wait=w0,
                            inc=(t8 == 7))
                    wsl[s] = emm_t[7]
                ev_sweep_res = []
                for t8 in range(8):
                    pp = self.ps(512 * t8, 512)
                    eb = p.op("tensor", "matmul", pp, self.ones_sb[0:1, :],
                              b2_bf[:, n * 512:n * 512 + 512],
                              start=False, stop=True,
                              wait=[emm_t[t8], ev_b2, self.ev_setup["ones"]],
                              inc=True)
                    t = half * 8 + t8
                    xt = self.x_sb[:, t * D + n * 512: t * D + n * 512 + 512]
                    er = p.op("vector", "tensor_tensor", xt, xt, pp, ALU.add,
                              wait=[eb, ev_res[t] if n == 1 else None], inc=True)
                    if n == 1 and stats_ok:
                        self.ln_stats(t, er)
                    ev_sweep_res.append(er)
                    ev_res[t] = er
                    ev_res_last = er
        self.prefetch_ffn = {}
        self.pop_scope()
        return ev_res


    def _bg_pump(self, n=1):
        while n > 0 and self._bg:
            if self._bg_skip > 0:
                self._bg_skip -= 1
                return
            self._bg.pop(0)()
            n -= 1

    def _bg_drain(self):
        while self._bg:
            self._bg.pop(0)()

    def _sa_new_state(self, ev0, ev1, pi, prev_o):
        return {"exp": [list(ev0), list(ev1)],
                "o_by_par": list(prev_o),
                "dance": None,
                "jpar": 0, "pi": pi}

    def _sa_pipeline(self, st, jlist, score_fn, o_fn, use_wt, wtb, wt_tile_fn,
                    first_o_waits, post_o=None):
        p = self.p
        nj = len(jlist)
        pi = st["pi"]
        pend = []
        for idx in range(nj + 1):
            if idx < nj:
                j = jlist[idx]
                par = st["jpar"]
                st["jpar"] ^= 1
                if use_wt:
                    eb = None
                    for hl in range(2):
                        eb = p.op("tensor", "matmul",
                                  self.ps(1024 * par + 512 * hl, 512),
                                  self.ident_sb[:], wt_tile_fn(j),
                                  start=True, stop=False,
                                  wait=[self.ev_ftw, self.ev_setup["ident"]]
                                       + st["exp"][0], inc=(hl == 1))
                    ev_s = score_fn(j, 1024 * par, [eb])
                else:
                    ev_s = score_fn(j, 1024 * par, list(st["exp"][0]))
                ev_e = p.op("scalar", "activation", self.pt[par][:],
                            self.ps(1024 * par, 1024), AF.Exp,
                            bias=self.nega[:, pi:pi + 1],
                            wait=[ev_s, st["o_by_par"][par]], inc=True)
                st["exp"] = [st["exp"][1], [ev_e]]
                gate = ev_e
                pend.append((idx, j, par, gate))
            self._bg_pump(1)
            if idx >= 1:
                (oidx, oj, opar, ogate) = pend.pop(0)
                extra = list(first_o_waits) if oidx == 0 else []
                evo = o_fn(oj, opar, ogate, extra, oidx == 0, oidx == nj - 1)
                st["o_by_par"][opar] = evo
                if post_o is not None:
                    post_o(oidx, oj, evo)
        return evo

    def o_dance(self, W, ev_oA, ev_oB):
        p = self.p
        nt = W // 128
        self._bg_pump(3)
        if self.dance_eng == "scalar":
            eevA = p.op("scalar", "activation", self.obuf[0:65, 0:W],
                        self.ps(2048, W)[0:65, :], AF.Copy,
                        wait=[ev_oA] + self.ev_obuf_war, inc=True)
            eevB = p.op("scalar", "activation", self.obuf[0:65, W:2 * W],
                        self.ps(2560, W)[0:65, :], AF.Copy,
                        wait=[ev_oB], inc=True)
        else:
            eevA = p.op("vector", "tensor_copy", self.obuf[0:65, 0:W],
                        self.ps(2048, W)[0:65, :],
                        wait=[ev_oA] + self.ev_obuf_war, inc=True)
            eevB = p.op("vector", "tensor_copy", self.obuf[0:65, W:2 * W],
                        self.ps(2560, W)[0:65, :], wait=[ev_oB, eevA], inc=True)
        etA = etB = None
        ident65 = self.ident32_sb[0:65, 0:65]
        for t in range(nt):
            etA = p.op("tensor", "transpose", self.ps(2048 + t * 128, 65),
                       self.obuf[0:65, t * 128:(t + 1) * 128], ident65,
                       wait=[eevA, eevB], inc=(t == nt - 1))
        for t in range(nt):
            etB = p.op("tensor", "transpose", self.ps(2560 + t * 128, 65),
                       self.obuf[0:65, W + t * 128: W + (t + 1) * 128],
                       ident65, wait=[eevB], inc=(t == nt - 1))
        self.ev_obuf_war = [etB]
        return etA, etB

    def _evac_of32(self, dst, dst_off, ntile, etA, etB, prev_read):
        p, psap = self.p, lambda off, dims: ap(self.PS, dims, off=off)
        erzA = p.op("vector", "reciprocal", self.rz[:, 0:ntile],
                    psap(2048 + 64, [[4096, 128], [128, ntile], [1, 1]]),
                    wait=[etA], inc=True)
        erzB = p.op("vector", "reciprocal", self.rz[:, 8:8 + ntile],
                    psap(2560 + 64, [[4096, 128], [128, ntile], [1, 1]]),
                    wait=[etB], inc=True)
        out = []
        for hl, (erz_h, po, rz0) in enumerate(((erzA, 2048, 0), (erzB, 2560, 8))):
            e = p.op("vector", "tensor_tensor",
                     ap(dst, [[dst.shape[1], 128], [128, ntile], [1, 64]],
                        off=dst_off + 64 * hl),
                     psap(po, [[4096, 128], [128, ntile], [1, 64]]),
                     ap(self.rz, [[16, 128], [1, ntile], [0, 64]], off=rz0),
                     ALU.mult, wait=[erz_h] + list(prev_read), inc=True)
            out.append(e)
        return out

    def _prep_bank(self, pp):
        pp["bk"] ^= 1
        base = 3072 + 512 * pp["bk"]
        war = pp["bank_ev"][pp["bk"]]
        return base, war

    def _stats_steps(self, steps, pp, out, pr, qf, kf):
        p = self.p
        pi = pr % 2

        def s_stat(sidx):
            hl, n = sidx // 4, sidx % 4
            b0 = 64 * hl
            base, war = self._prep_bank(pp)
            e = p.op("tensor", "matmul", self.ps(base, 512),
                     kf[b0:b0 + 64, 0:2048:16],
                     qf[b0:b0 + 64, n * 512:n * 512 + 512],
                     start=True, stop=True, tile_position=(b0, 0),
                     wait=[out["qe"], out["ke"]] + war, inc=True)
            er = p.op("vector", "reduce_max",
                      self.smax[:, 8 * pi + sidx: 8 * pi + sidx + 1],
                      self.ps(base, 512), axis=AX.X,
                      wait=[e] + self.smax_war[pi], inc=True)
            pp["bank_ev"][pp["bk"]] = [er]
            pp["st8"] = er
        for sidx in range(8):
            steps.append(lambda sidx=sidx: s_stat(sidx))

        def s_comb():
            er2 = p.op("vector", "reduce_max", self.smax2[:, pi:pi + 1],
                       self.smax[:, 8 * pi:8 * pi + 8], axis=AX.X,
                       wait=[pp["st8"]] + self.smax2_war[pi], inc=True)
            self.smax_war[pi] = [er2]
            base, war = self._prep_bank(pp)
            etr = p.op("tensor", "transpose", self.ps(base, 128)[0:1, :],
                       self.smax2[:, pi:pi + 1], self.ident32_sb[:],
                       wait=[er2, self.ev_setup["ident32"]] + war, inc=True)
            erow = p.op("vector", "reduce_max", self.smax2[0:1, pi:pi + 1],
                        self.ps(base, 128)[0:1, :], axis=AX.X,
                        wait=[etr], inc=True)
            ecb = p.op("vector", "tensor_copy", self.sa1b[:, pi:pi + 1],
                       self.smax2[0:1, pi:pi + 1],
                       wait=[erow] + self.sa1b_war[pi], inc=True)
            self.smax2_war[pi] = [etr, ecb]
            ebc = p.op("tensor", "matmul", self.ps(base + 128, 1)[0:128, 0:1],
                       self.ones_sb[0:1, :], self.sa1b[:, pi:pi + 1],
                       start=True, stop=True,
                       wait=[ecb, self.ev_setup["ones"]], inc=True)
            self.sa1b_war[pi] = [ebc]
            ev_nega = p.op("scalar", "activation", self.nega[:, pi:pi + 1],
                           self.ps(base + 128, 1)[:, 0:1], AF.Copy, scale=-1.0,
                           wait=[ebc] + self.pair_done[pi], inc=True)
            pp["bank_ev"][pp["bk"]] = [ev_nega]
            self.prep_last = [ev_nega]
            out["nega"] = ev_nega
            out["erow"] = erow
        steps.append(s_comb)

    def _prep_steps_a(self, pr, evl):
        p, d = self.p, self.d
        pi = pr % 2
        wbuf = self.wb[pi]
        qf, kf, vpair = self.qf[pi], self.kf[pi], self.vpair[pi]
        pp = {"bk": 0, "bank_ev": [list(self.prep_last), list(self.prep_last)]}
        out = {}
        self.prep_out[pr] = out
        steps = []

        pf = (pr == 0 and self.prefetch.get("sl") == "attn_a")

        def s_stream():
            if pf:
                pp["wv"] = self.prefetch["wv"]
                pp["wq"] = self.stream_block(d["wq_a"], pr * D, 8, 128, wbuf,
                                             "gpsimd", f"gwq{pi}",
                                             wait=self.wbl[pi]
                                             + self.pair_done[pi])
            else:
                pp["wv"] = self.stream_block(d["wv_a"], pr * D, 8, 128, wbuf,
                                             "gpsimd", f"gwv{pi}", col0=2048,
                                             wait=self.wbl[pi]
                                             + self.pair_done[pi])
                pp["wq"] = self.stream_block(d["wq_a"], pr * D, 8, 128, wbuf,
                                             "gpsimd", f"gwq{pi}")
            pp["wk"] = self.stream_block(d["wk_a"], pr * D, 8, 128, wbuf,
                                         "gpsimd", f"gwk{pi}", col0=1024)
            pp["ones"] = p.op("gpsimd", "memset", vpair[:], 1.0,
                              wait=self.pair_done[pi], inc=True)
        steps.append(s_stream)

        def s_v(t):
            base, war = self._prep_bank(pp)
            emv = None
            for k in range(8):
                emv = p.op("tensor", "matmul", self.ps(base, 128),
                           self.xh_fm[:, k * T + t * 128: k * T + t * 128 + 128],
                           wbuf[:, 2048 + k * 128: 2048 + k * 128 + 128],
                           start=(k == 0), stop=(k == 7),
                           wait=[pp["wv"], evl] + war, inc=(k == 7))
            dst = ap(vpair, [[2080, 128], [65, 2], [1, 64]], off=t * 130)
            srcp = ap(self.PS, [[4096, 128], [64, 2], [1, 64]], off=base)
            ev = p.op("vector", "tensor_scalar", dst, srcp, 1.0 / 3.0, None,
                      ALU.mult, wait=[emv, pp["ones"]], inc=True)
            pp["bank_ev"][pp["bk"]] = [ev]
            out["vdone"] = ev
        for t in range(NT):
            steps.append(lambda t=t: s_v(t))

        def s_q(n, col0, wev, dstbuf, scale, key):
            base, war = self._prep_bank(pp)
            emm = None
            for k in range(8):
                emm = p.op("tensor", "matmul", self.ps(base, 512),
                           wbuf[:, col0 + k * 128: col0 + k * 128 + 128],
                           self.xh_fm[:, k * T + n * 512: k * T + n * 512 + 512],
                           start=(k == 0), stop=(k == 7),
                           wait=[pp[wev], evl] + war, inc=(k == 7))
            ev = p.op("vector", "tensor_scalar",
                      dstbuf[:, n * 512:n * 512 + 512],
                      self.ps(base, 512), scale, None, ALU.mult,
                      wait=[emm] + self.pair_done[pi], inc=True)
            pp["bank_ev"][pp["bk"]] = [ev]
            self.wbl[pi] = [emm]
            out[key] = ev
        for n in range(4):
            steps.append(lambda n=n: s_q(n, 0, "wq", qf, 0.125, "qe"))
        for n in range(4):
            steps.append(lambda n=n: s_q(n, 1024, "wk", kf, 1.0, "ke"))

        self._stats_steps(steps, pp, out, pr, qf, kf)
        return steps

    def _prep_steps_b(self, pr, evl, latent, qtmp, ropec, ropes, rt0, ev_le):
        p, d = self.p, self.d
        pi = pr % 2
        wbuf = self.wb[pi]
        qf, kf, vpair = self.qf[pi], self.kf[pi], self.vpair[pi]
        pp = {"bk": 0, "bank_ev": [list(self.prep_last), list(self.prep_last)]}
        out = {}
        self.prep_out[pr] = out
        steps = []

        pf = (pr == 0 and self.prefetch.get("sl") == "attn_b")
        kc0, vc0 = (2048, 2304) if pf else (1024, 1280)

        def s_stream():
            if pf:
                pp["wk"] = self.prefetch["wk"]
                pp["wv"] = self.prefetch["wv"]
                pp["wq"] = self.stream_block(d["wq_b"], pr * D, 8, 128, wbuf,
                                             "gpsimd", f"gwq{pi}",
                                             wait=self.wbl[pi]
                                             + self.pair_done[pi])
            else:
                pp["wv"] = self.stream_block(d["wkvu_v"], pr * 256, 2, 128,
                                             wbuf, "gpsimd", f"gwv{pi}",
                                             col0=1280, wait=self.wbl[pi]
                                             + self.pair_done[pi])
                pp["wq"] = self.stream_block(d["wq_b"], pr * D, 8, 128, wbuf,
                                             "gpsimd", f"gwq{pi}")
                pp["wk"] = self.stream_block(d["wkvu_k"], pr * 256, 2, 128,
                                             wbuf, "gpsimd", f"gwk{pi}",
                                             col0=1024)
            pp["ones"] = p.op("gpsimd", "memset", vpair[:], 1.0,
                              wait=self.pair_done[pi], inc=True)
        steps.append(s_stream)

        def s_v(t):
            base, war = self._prep_bank(pp)
            for kc in range(2):
                p.op("tensor", "matmul", self.ps(base, 128),
                     latent[:, kc * T + t * 128: kc * T + t * 128 + 128],
                     wbuf[:, vc0 + kc * 128: vc0 + kc * 128 + 128],
                     start=(kc == 0), stop=False,
                     wait=[pp["wv"], ev_le] + war, inc=False)
            emv = p.op("tensor", "matmul", self.ps(base, 128),
                       self.ones_sb[0:1, :],
                       self.bkuv_bf[:, pr * 128: pr * 128 + 128],
                       start=False, stop=True,
                       wait=[self.ev_setup["bkuv"], self.ev_setup["ones"]],
                       inc=True)
            dst = ap(vpair, [[2080, 128], [65, 2], [1, 64]], off=t * 130)
            srcp = ap(self.PS, [[4096, 128], [64, 2], [1, 64]], off=base)
            ev = p.op("vector", "tensor_copy", dst, srcp,
                      wait=[emv, pp["ones"]], inc=True)
            pp["bank_ev"][pp["bk"]] = [ev]
            out["vdone"] = ev
        for t in range(NT):
            steps.append(lambda t=t: s_v(t))

        def s_qk_mm(n):
            baseq, warq = self._prep_bank(pp)
            emq = None
            for k in range(8):
                emq = p.op("tensor", "matmul", self.ps(baseq, 512),
                           wbuf[:, k * 128:(k + 1) * 128],
                           self.xh_fm[:, k * T + n * 512: k * T + n * 512 + 512],
                           start=(k == 0), stop=(k == 7),
                           wait=[pp["wq"], evl] + warq, inc=(k == 7))
            basek, wark = self._prep_bank(pp)
            emk = None
            for kc in range(2):
                emk = p.op("tensor", "matmul", self.ps(basek, 512),
                           wbuf[:, kc0 + kc * 128: kc0 + kc * 128 + 128],
                           latent[:, kc * T + n * 512: kc * T + n * 512 + 512],
                           start=(kc == 0), stop=(kc == 1),
                           wait=[pp["wk"], ev_le] + wark, inc=(kc == 1))
            pp["qk"] = (baseq, emq, basek, emk)
            self.wbl[pi] = [emq, emk]

        def s_tab(cc, wait):
            p.op("sync", "dma_start", ropec,
                 ap(d["ropeC"], [[T, 128], [1, 256]], off=cc * 256),
                 wait=wait, dsem="rtab", inc=True)
            return p.op("sync", "dma_start", ropes,
                        ap(d["ropeS"], [[T, 128], [1, 256]], off=cc * 256),
                        dsem="rtab", inc=True)

        def rope_args(which):
            baseq, emq, basek, emk = pp["qk"]
            if which == 0:
                return baseq, emq, qf, 0.125, self.bq8_sb[:, pr:pr + 1], (0, 1)
            return basek, emk, kf, 1.0, self.bkuk_sb[:, pr:pr + 1], (1, 0)

        def s_rope_ev(n, which):
            base, ev_mm, dst, sc, bias, subs = rope_args(which)
            prev = pp.get("rope_prev", self.rope_last) + self.pair_done[pi]
            pp["r_prev"] = prev
            bufs = {}
            ev0 = None
            for sub, buf in ((subs[0], qtmp), (subs[1], rt0)):
                ev0 = p.op("vector", "tensor_scalar", buf,
                           self.ps(base + sub * 256, 256), sc, bias,
                           ALU.mult, ALU.add,
                           wait=[ev_mm, ev0, self.ev_setup["bq8"],
                                 self.ev_setup["bkuk"]] + prev, inc=True)
                bufs[sub] = buf
            pp["r_bufs"] = bufs
            pp["r_ev0"] = ev0

        def s_rope_rot(n, which):
            base, ev_mm, dst, sc, bias, subs = rope_args(which)
            emr = None
            for i, sub in enumerate(subs):
                emr = p.op("tensor", "matmul", self.ps(base + sub * 256, 256),
                           self.rotp_sb[:], pp["r_bufs"][sub],
                           start=(i == 0), stop=(i == 1),
                           wait=[pp["r_ev0"], self.ev_setup["rotp"]],
                           inc=(i == 1))
            pp["r_rot"] = emr

        def s_rope_tt(n, which):
            base, ev_mm, dst, sc, bias, subs = rope_args(which)
            emr = pp["r_rot"]
            bufs = pp["r_bufs"]
            e3 = None
            for i, sub in enumerate(subs):
                cc = 2 * n + sub
                if which == 0 or i == 1:
                    ev_tab = s_tab(cc, [e3] if e3 else pp["r_prev"])
                    pp["tab"] = ev_tab
                e1 = p.op("vector", "tensor_tensor",
                          self.ps(base + sub * 256, 256),
                          self.ps(base + sub * 256, 256), ropes,
                          ALU.mult, wait=[emr, pp["tab"]], inc=True)
                e2 = p.op("vector", "tensor_tensor", bufs[sub],
                          bufs[sub], ropec, ALU.mult,
                          wait=[e1], inc=True)
                e3 = p.op("vector", "tensor_tensor",
                          dst[:, cc * 256:cc * 256 + 256],
                          self.ps(base + sub * 256, 256), bufs[sub],
                          ALU.add, wait=[e2] + self.pair_done[pi], inc=True)
            pp["rope_prev"] = [e3]
            self.rope_last = [e3]
            pp["bank_ev"][(base - 3072) // 512] = [e3]
            out["qe" if which == 0 else "ke"] = e3

        for n in range(4):
            steps.append(lambda n=n: s_qk_mm(n))
            for which in range(2):
                steps.append(lambda n=n, w=which: s_rope_ev(n, w))
                steps.append(lambda n=n, w=which: s_rope_rot(n, w))
                steps.append(lambda n=n, w=which: s_rope_tt(n, w))

        self._stats_steps(steps, pp, out, pr, qf, kf)
        return steps

    def attn_a(self, ev_ln):
        p, d = self.p, self.d
        self.push_scope()
        self.alloc_attn_common()
        self.obank_war = [ev_ln[-1]]
        self.dance_eng = "vector"
        vdil = self.ssb("vdil", [128, 16 * 130], BF16)
        odg = self.ssb("odg", [128, 2048], BF16)
        odt = self.ssb("odt", [128, 2048], BF16)
        ev_dil_war = None
        self.ev_odg_war = list(self.entry)
        self.ev_odt_war = list(self.entry)
        ev_res = []

        self._bg = self._prep_steps_a(0, ev_ln[-1])
        for pr in range(NP):
            pi = pr % 2
            self._bg_drain()
            if pr + 1 < NP:
                self._bg = self._prep_steps_a(pr + 1, ev_ln[-1])
                self._bg_skip = 10
            po = self.prep_out[pr]
            qf, kf, vpair = self.qf[pi], self.kf[pi], self.vpair[pi]

            st = self._sa_new_state([po["nega"]] + self.attn_prev0,
                                    [po["nega"]] + self.attn_prev1, pi,
                                    self.prev_o)
            ev_glb = []
            first_sco = [po["erow"]]
            dance_prev = {"ev": list(self.obank_war)}

            def score_fn(jj, psb, waits, fs=first_sco):
                c, jt = jj // 16, jj % 16
                esA = p.op("tensor", "matmul", self.ps(psb, 512),
                           kf[0:64, jt * 128:(jt + 1) * 128],
                           qf[0:64, c * 512:c * 512 + 512],
                           start=True, stop=True, tile_position=(0, 0),
                           wait=waits + fs, inc=False)
                esB = p.op("tensor", "matmul", self.ps(psb + 512, 512),
                           kf[64:128, jt * 128:(jt + 1) * 128],
                           qf[64:128, c * 512:c * 512 + 512],
                           start=True, stop=True, tile_position=(64, 0),
                           inc=True)
                if fs:
                    fs.clear()
                return esB

            def o_fn(jj, par, gate, extra, first, last):
                jt = jj % 16
                w = [gate] + extra + (dance_prev["ev"] if jt == 0 else [])
                evo = None
                for hl, po_ in ((0, 2048), (1, 2560)):
                    evo = p.op("tensor", "matmul", self.ps(po_, 512)[0:65, :],
                               vpair[:, jt * 130 + 65 * hl: jt * 130 + 65 * hl + 65],
                               self.pt[par][:, 512 * hl: 512 * hl + 512],
                               start=(jt == 0), stop=(jt == 15),
                               wait=w, inc=(hl == 1))
                return evo

            def post_o(oidx, jj, evo):
                if jj % 16 != 15:
                    return
                c = jj // 16
                etA, etB = self.o_dance(512, evo, evo)
                es = self._evac_of32(self.of32, 512 * c, 4, etA, etB,
                                     self.ev_of32_war)
                ev_glb.extend(es)
                dance_prev["ev"] = es

            self._sa_pipeline(st, list(range(64)), score_fn, o_fn,
                              False, None, None, [po["vdone"]], post_o=post_o)

            ev_b = p.op("sync", "dma_start", d["vp_d"].ap(), vpair[:],
                        wait=[po["vdone"], ev_dil_war]
                             + (list(self.entry) if pr == 0 else []),
                        dsem="dil", inc=True)
            ev_gather = None
            for ph in range(4):
                srcap = ap(d["vp_d"], [[8320, 32], [2080, 4], [520, 4], [1, 130]],
                           off=130 * ph)
                dstap = ap(vdil, [[2080, 32], [520, 4], [130, 4], [1, 130]],
                           off=ph * 32 * 2080)
                ev_gather = p.op("sync", "dma_start", dstap, srcap,
                                 wait=[ev_b], dsem="dil", inc=True)
            ev_dil = []

            def score_fn_d(jj, psb, waits):
                g, j4 = jj // 4, jj % 4
                esA = p.op("tensor", "matmul", self.ps(psb, 512),
                           kf[0:64, 512 * j4 + g: 512 * (j4 + 1): 4],
                           qf[0:64, g: 2048: 4],
                           start=True, stop=True, tile_position=(0, 0),
                           wait=waits, inc=False)
                esB = p.op("tensor", "matmul", self.ps(psb + 512, 512),
                           kf[64:128, 512 * j4 + g: 512 * (j4 + 1): 4],
                           qf[64:128, g: 2048: 4],
                           start=True, stop=True, tile_position=(64, 0),
                           inc=True)
                return esB

            def o_fn_d(jj, par, gate, extra, first, last):
                g, j4 = jj // 4, jj % 4
                w = [gate, ev_gather] + extra \
                    + (dance_prev["ev"] if j4 == 0 else [])
                evo = None
                for hl, po_ in ((0, 2048), (1, 2560)):
                    evo = p.op("tensor", "matmul", self.ps(po_, 512)[0:65, :],
                               vdil[:, (4 * g + j4) * 130 + 65 * hl:
                                    (4 * g + j4) * 130 + 65 * hl + 65],
                               self.pt[par][:, 512 * hl: 512 * hl + 512],
                               start=(j4 == 0), stop=(j4 == 3),
                               wait=w, inc=(hl == 1))
                return evo

            def post_o_d(oidx, jj, evo):
                if jj % 4 != 3:
                    return
                g = jj // 4
                etA, etB = self.o_dance(512, evo, evo)
                es = self._evac_of32(odg, 512 * g, 4, etA, etB,
                                     [ev_dil[-1]] if ev_dil
                                     else self.ev_odg_war)
                ev_dil.extend(es)
                dance_prev["ev"] = es

            ev_last_do = self._sa_pipeline(st, list(range(16)), score_fn_d,
                                           o_fn_d, False, None, None, [],
                                           post_o=post_o_d)
            ev_dil_war = ev_last_do
            ev_s1 = None
            for ph in range(4):
                srcap = ap(odg, [[2048, 32], [512, 4], [128, 4], [1, 128]],
                           off=32 * ph * 2048)
                dstap = ap(d["od_d"], [[4 * 2048, 32], [2048, 4], [512, 4], [1, 128]],
                           off=128 * ph)
                ev_s1 = p.op("sync", "dma_start", dstap, srcap,
                             wait=[ev_dil[-1]], dsem="odil", inc=True)
            ev_scat = p.op("sync", "dma_start", odt[:], d["od_d"].ap(),
                           wait=[ev_s1] + self.ev_odt_war, dsem="odil", inc=True)
            ev_dadd = p.op("vector", "tensor_tensor", self.of32[:], self.of32[:],
                           odt[:], ALU.add, wait=[ev_scat, ev_glb[-1]], inc=True)
            self.ev_odt_war = [ev_dadd]

            loc = {"add": [ev_dadd], "scat": [ev_s1]}

            def score_fn_l(wg, psb, waits):
                esA = esB = None
                for wi in range(4):
                    w = 4 * wg + wi
                    esA = p.op("tensor", "matmul",
                               self.ps(psb + wi * 128, 128),
                               kf[0:64, w * 128:(w + 1) * 128],
                               qf[0:64, w * 128:(w + 1) * 128],
                               start=True, stop=True, tile_position=(0, 0),
                               wait=waits, inc=False)
                    esB = p.op("tensor", "matmul",
                               self.ps(psb + 512 + wi * 128, 128),
                               kf[64:128, w * 128:(w + 1) * 128],
                               qf[64:128, w * 128:(w + 1) * 128],
                               start=True, stop=True, tile_position=(64, 0),
                               inc=(wi == 3))
                return esB

            def o_fn_l(wg, par, gate, extra, first, last):
                w0 = [gate] + extra + dance_prev["ev"]
                evo = None
                for hl, po_ in ((0, 2048), (1, 2560)):
                    for wi in range(4):
                        w = 4 * wg + wi
                        evo = p.op("tensor", "matmul",
                                   self.ps(po_ + wi * 128, 128)[0:65, :],
                                   vpair[:, w * 130 + 65 * hl:
                                         w * 130 + 65 * hl + 65],
                                   self.pt[par][:, 512 * hl + wi * 128:
                                              512 * hl + wi * 128 + 128],
                                   start=(wi == 0), stop=(wi == 3),
                                   wait=w0, inc=(hl == 1 and wi == 3))
                return evo

            def post_o_l(oidx, wg, evo):
                etA, etB = self.o_dance(512, evo, evo)
                e1, e2 = self._evac_of32(odg, 512 * wg, 4, etA, etB,
                                         loc["scat"])
                loc["add"] = [p.op("vector", "tensor_tensor",
                                   ap(self.of32, [[2048, 128], [1, 512]],
                                      off=wg * 512),
                                   ap(self.of32, [[2048, 128], [1, 512]],
                                      off=wg * 512),
                                   ap(odg, [[2048, 128], [1, 512]],
                                      off=wg * 512),
                                   ALU.add, wait=[e1, e2] + loc["add"],
                                   inc=True)]
                dance_prev["ev"] = [e1, e2]

            self._sa_pipeline(st, list(range(4)), score_fn_l, o_fn_l,
                              False, None, None, [], post_o=post_o_l)
            ev_loc_add = loc["add"][0]
            self.ev_odg_war = [ev_loc_add]
            self.obank_war = [ev_loc_add]

            ev_evac2 = self._pair_tail(pr, st, qf, kf, [ev_loc_add])
        self.wo_phase(d["w_out"], None, ev_evac2, self.wbl, ev_res)
        self.pop_scope()
        return ev_res

    def _pair_tail(self, pr, st, qf, kf, ev_done, extra_prev=()):
        p = self.p
        pi = pr % 2
        ecast0 = p.op("vector", "tensor_scalar", kf[:, 0:1024],
                      self.of32[:, 0:1024], 1.0, None, ALU.mult,
                      wait=ev_done + self.ev_cast_war + st["o_by_par"],
                      inc=True)
        ecast1 = p.op("vector", "tensor_scalar", qf[:, 0:1024],
                      self.of32[:, 1024:2048], 1.0, None, ALU.mult,
                      wait=[ecast0], inc=True)
        srcs = (kf, qf)
        casts = (ecast0, ecast1)
        war_tr = list(st["exp"][0]) + list(st["exp"][1]) \
            + st.get("extra_tr_war", [])
        ev_evac2 = None
        for cc in range(2):
            etr2 = None
            for tt in range(8):
                etr2 = p.op("tensor", "transpose",
                            self.ps(1024 + cc * 512 + tt * 64, 64).bitcast(BF16),
                            srcs[cc][:, tt * 128:(tt + 1) * 128],
                            self.ident_sb[:],
                            wait=[casts[cc]] + war_tr,
                            inc=(tt == 7))
            war_tr = []
            ev_evac2 = p.op("scalar", "activation",
                            self.ofm[:, pr * T + cc * 1024:
                                     pr * T + cc * 1024 + 1024],
                            self.ps(1024 + cc * 512, 512).bitcast(BF16),
                            AF.Copy, wait=[etr2], inc=True)
        self.ev_cast_war = []
        self.ev_of32_war = [ecast1]
        self.pair_done[pi] = [ev_evac2]
        lexp = list(st["exp"][1]) + list(extra_prev)
        self.attn_prev0 = lexp
        self.attn_prev1 = lexp + [ev_evac2]
        self.prev_o = list(st["o_by_par"])
        return ev_evac2

    def attn_b(self, ev_ln):
        p, d = self.p, self.d
        self.push_scope()
        self.alloc_attn_common()
        self.obank_war = [ev_ln[-1]]
        self.dance_eng = "scalar"
        latent = self.ssb("latent", [128, 2 * T], BF16)
        qtmp = self.xt32[:, 0:256]
        rt0 = self.xt32[:, 256:512]
        ropec = self.xt32[:, 512:768]
        ropes = self.xt32[:, 768:1024]
        ftw = self.ssb("ftw", [128, 3968], BF16)
        sig = self.ssb("sig", [128, 32], F32)
        mvrow_bf = self.ssb("mvrow_bf", [1, 130], BF16)
        mkb = self.ssb("mkb", [128, 1], BF16)
        self.ev_ftw = p.op("gpsimd", "dma_start", ftw[:], d["ftp"].ap(),
                           wait=[ev_ln[-1]], dsem="ftw", inc=True)
        ev_res = []

        ev_wd = self.stream_block(d["wkvd"], 0, 8, 256, self.wb[0], "gpsimd",
                                  "glat", wait=list(self.entry) + self.wbl[0])
        ev_le = None
        for m in range(2):
            emm = None
            for k in range(8):
                lhs = self.wb[0][:, k * 256 + m * 128: k * 256 + m * 128 + 128]
                for n in range(4):
                    emm = p.op("tensor", "matmul", self.ps(m * 2048 + n * 512, 512),
                               lhs,
                               self.xh_fm[:, k * T + n * 512: k * T + n * 512 + 512],
                               start=(k == 0), stop=(k == 7),
                               wait=[ev_wd, ev_ln[-1]], inc=(k == 7 and n == 3))
            ev_le = p.op("scalar", "activation", latent[:, m * T:(m + 1) * T],
                         self.ps(m * 2048, 2048), AF.Identity,
                         bias=self.bkvd_sb[:, m:m + 1],
                         wait=[emm, self.ev_setup["bkvd"]], inc=True)
        self.wbl[0] = [emm]
        self.prep_last = [ev_le]
        self.attn_prev0 = [ev_le]
        self.attn_prev1 = [ev_le]
        self.obank_war = [ev_le]

        self._bg = self._prep_steps_b(0, ev_ln[-1], latent, qtmp, ropec,
                                      ropes, rt0, ev_le)
        for pr in range(NP):
            pi = pr % 2
            self._bg_drain()
            if pr + 1 < NP:
                self._bg = self._prep_steps_b(pr + 1, ev_ln[-1], latent, qtmp,
                                              ropec, ropes, rt0, ev_le)
                self._bg_skip = 10
            po = self.prep_out[pr]
            qf, kf, vpair = self.qf[pi], self.kf[pi], self.vpair[pi]

            st = self._sa_new_state([po["nega"]] + self.attn_prev0,
                                    [po["nega"]] + self.attn_prev1, pi,
                                    self.prev_o)
            ev_glb = []
            first_sco = [po["erow"]]
            dance_prev = {"ev": list(self.obank_war)}

            def score_fn(jj, psb, waits, fs=first_sco):
                c, jt = jj // 16, jj % 16
                esA = p.op("tensor", "matmul", self.ps(psb, 512),
                           kf[0:64, jt * 128:(jt + 1) * 128],
                           qf[0:64, c * 512:c * 512 + 512],
                           start=False, stop=True, tile_position=(0, 0),
                           wait=waits + fs, inc=False)
                esB = p.op("tensor", "matmul", self.ps(psb + 512, 512),
                           kf[64:128, jt * 128:(jt + 1) * 128],
                           qf[64:128, c * 512:c * 512 + 512],
                           start=False, stop=True, tile_position=(64, 0),
                           inc=True)
                if fs:
                    fs.clear()
                return esB

            def wt_tile_fn(jj):
                c, jt = jj // 16, jj % 16
                off = 1920 + 512 * c - 128 * jt
                return ftw[:, off:off + 512]

            def o_fn(jj, par, gate, extra, first, last):
                jt = jj % 16
                w = [gate] + extra + (dance_prev["ev"] if jt == 0 else [])
                evo = None
                for hl, po_ in ((0, 2048), (1, 2560)):
                    evo = p.op("tensor", "matmul", self.ps(po_, 512)[0:65, :],
                               vpair[:, jt * 130 + 65 * hl: jt * 130 + 65 * hl + 65],
                               self.pt[par][:, 512 * hl: 512 * hl + 512],
                               start=(jt == 0), stop=(jt == 15),
                               wait=w, inc=(hl == 1))
                return evo

            def post_o(oidx, jj, evo):
                if jj % 16 != 15:
                    return
                c = jj // 16
                etA, etB = self.o_dance(512, evo, evo)
                es = self._evac_of32(self.of32, 512 * c, 4, etA, etB,
                                     self.ev_of32_war)
                ev_glb.extend(es)
                dance_prev["ev"] = es

            self._sa_pipeline(st, list(range(64)), score_fn, o_fn,
                              True, None, wt_tile_fn, [po["vdone"]],
                              post_o=post_o)

            emk2 = p.op("vector", "reduce_sum", self.rz[:, 4:5], kf[:],
                        axis=AX.X, wait=[ev_glb[-1]], inc=True)
            emk3 = p.op("vector", "tensor_scalar", self.rz[:, 5:6],
                        self.rz[:, 4:5], 1.0 / T, None, ALU.mult,
                        wait=[emk2], inc=True)
            emk4 = p.op("vector", "tensor_copy", mkb[:], self.rz[:, 5:6],
                        wait=[emk3], inc=True)
            ems = None
            for hl in range(2):
                b0 = 64 * hl
                for it in range(NT):
                    ems = p.op("tensor", "matmul",
                               self.ps(16 * hl + it, 1)[0:128, 0:1],
                               qf[b0:b0 + 64, it * 128:(it + 1) * 128],
                               mkb[b0:b0 + 64, 0:1],
                               start=(hl == 0 and it == 0),
                               stop=(hl == 1 and it == NT - 1),
                               tile_position=(b0, 0),
                               wait=[emk4] + st["exp"][1],
                               inc=(hl == 1 and it == NT - 1))
            esig = p.op("scalar", "activation", sig[:], self.ps(0, 32),
                        AF.Exp, scale=-8.0, wait=[ems], inc=True)
            esig2 = p.op("vector", "tensor_scalar", sig[:], sig[:], 1.0, None,
                         ALU.add, wait=[esig], inc=True)
            esig3 = p.op("vector", "reciprocal", sig[:], sig[:],
                         wait=[esig2], inc=True)
            esig4 = p.op("vector", "tensor_scalar", sig[:], sig[:], 0.2, None,
                         ALU.mult, wait=[esig3], inc=True)
            emv = emve = None
            for hl in range(2):
                for jt in range(NT):
                    emv = p.op("tensor", "matmul",
                               self.ps(512 * (1 + hl), 65)[0:1, 0:65],
                               self.ones_sb[:, 0:1],
                               vpair[:, jt * 130 + 65 * hl: jt * 130 + 65 * hl + 65],
                               start=(jt == 0), stop=(jt == NT - 1),
                               wait=[po["vdone"]] + st["exp"][1],
                               inc=(jt == NT - 1))
                emve = p.op("scalar", "activation",
                            mvrow_bf[0:1, 65 * hl:65 * hl + 65],
                            self.ps(512 * (1 + hl), 65)[0:1, 0:65], AF.Copy,
                            scale=1.0 / T, wait=[emv], inc=True)
            ebc2 = None
            for hl in range(2):
                ebc2 = p.op("tensor", "matmul",
                            self.ps(1536 + 64 * hl, 64)[0:128, 0:64],
                            self.ones_sb[0:1, :],
                            mvrow_bf[:, 65 * hl: 65 * hl + 64],
                            start=(hl == 0), stop=(hl == 1),
                            wait=[emve, ems] + st["exp"][1], inc=(hl == 1))
            ein = None
            for hl in range(2):
                e1 = p.op("vector", "tensor_tensor",
                          ap(kf, [[2048, 128], [64, 16], [1, 64]], off=1024),
                          ap(self.PS, [[4096, 128], [0, 16], [1, 64]],
                             off=1536 + 64 * hl),
                          ap(sig, [[32, 128], [1, 16], [0, 64]], off=16 * hl),
                          ALU.mult,
                          wait=[ebc2, esig4, st["o_by_par"][0], st["o_by_par"][1],
                                ein], inc=True)
                ein = p.op("vector", "tensor_tensor",
                           ap(self.of32, [[2048, 128], [128, 16], [1, 64]],
                              off=64 * hl),
                           ap(self.of32, [[2048, 128], [128, 16], [1, 64]],
                              off=64 * hl),
                           ap(kf, [[2048, 128], [64, 16], [1, 64]], off=1024),
                           ALU.add, wait=[e1], inc=True)

            st["extra_tr_war"] = [esig]
            self.obank_war = [ein]
            ev_evac2 = self._pair_tail(pr, st, qf, kf, [ein], extra_prev=[emve])
        self.wo_phase(d["wo_b"], self.bo_bf, ev_evac2, self.wbl, ev_res)
        self.pop_scope()
        return ev_res

    def wo_phase(self, wdram, bias_bf, ev_pair_done, wbl, ev_res):
        p = self.p
        self.stream_block(wdram, 0, 6, 512, self.wb[0], "gpsimd", "wo",
                          wait=wbl[0])
        self.stream_block(wdram, 6 * 128, 6, 512, self.wb[1], "gpsimd", "wo",
                          wait=wbl[1])
        self.stream_block(wdram, 12 * 128, 2, 512, self.pt[0], "gpsimd", "wo",
                          wait=[self.prev_o[0], self.prev_o[1]])
        ev_w = self.stream_block(wdram, 14 * 128, 2, 512, self.pt[1], "gpsimd", "wo")

        def blk(b):
            if b < 6:
                return self.wb[0][:, b * 512:(b + 1) * 512]
            if b < 12:
                return self.wb[1][:, (b - 6) * 512:(b - 5) * 512]
            if b < 14:
                return self.pt[0][:, (b - 12) * 512:(b - 11) * 512]
            return self.pt[1][:, (b - 14) * 512:(b - 13) * 512]

        emm_prev = [ev_pair_done, ev_pair_done]
        for t in range(NT):
            pp0 = 2048 * (t % 2)
            eb = None
            for k in range(8):
                lhs = self.ofm[:, k * T + t * 128: k * T + t * 128 + 128]
                for n in range(2):
                    last = (bias_bf is None and k == 7)
                    eb = p.op("tensor", "matmul", self.ps(pp0 + n * 512, 512),
                              lhs, blk(k * 2 + n), start=(k == 0), stop=last,
                              wait=[ev_w, ev_pair_done, emm_prev[t % 2]],
                              inc=(last and n == 1))
            if bias_bf is not None:
                for n in range(2):
                    eb = p.op("tensor", "matmul", self.ps(pp0 + n * 512, 512),
                              self.ones_sb[0:1, :],
                              bias_bf[:, n * 512: n * 512 + 512],
                              start=False, stop=True,
                              wait=[self.ev_setup["bo"], self.ev_setup["ones"]],
                              inc=(n == 1))
            xt = self.x_sb[:, t * D:(t + 1) * D]
            er = p.op("vector", "tensor_tensor", xt, xt, self.ps(pp0, 1024),
                      ALU.add, wait=[eb], inc=True)
            self.ln_stats(t, er)
            emm_prev[t % 2] = er
            ev_res.append(er)
        for s2 in range(2):
            self.prefetch_ffn[f"w{s2}"] = self.stream_block(
                self.d["w1r"], s2 * D, 8, 128, self.wb[0], "gpsimd",
                f"pw{s2}", col0=s2 * 1024, wait=[eb])
        return ev_res

    def build(self):
        self.setup()
        ev_store = ()
        for s in range(self.nseq):
            evl = self.load_x(s, extra_wait=ev_store)
            for t in range(NT):
                self.ln_stats(t, evl[t % 4])
            ev_x = [evl[-1]] * NT
            for i, sl in enumerate(self.sublayers):
                if i + 1 < len(self.sublayers):
                    nxt = self.sublayers[i + 1]
                elif s + 1 < self.nseq:
                    nxt = self.sublayers[0]
                else:
                    nxt = None
                self.entry = self.p.snapshot()
                ev_ln = self.layernorm(ev_x)
                if sl == "ffn":
                    ev_x = self.ffn(ev_ln, nxt=nxt,
                                    stats_ok=(i + 1 < len(self.sublayers)))
                elif sl == "attn_a":
                    ev_x = self.attn_a(ev_ln)
                elif sl == "attn_b":
                    ev_x = self.attn_b(ev_ln)
            ev_store = self.store_x(s, ev_x)
        self.p.op("tensor", None, wait=ev_store)
        self.p.barrier()
        self.p.emit()
        self.es.close()
        return self.nc


N_CORES = 8


def kernel(**inputs):
    m = MK(sublayers=("attn_a", "ffn", "attn_b", "ffn"), nseq=2)
    nc = m.build()
    in_maps = [host_inputs(inputs, core) for core in range(N_CORES)]
    from concourse.bass_utils import run_bass_kernel_spmd
    res = run_bass_kernel_spmd(nc, in_maps, core_ids=list(range(N_CORES)))
    out = np.concatenate([res.results[c]["x_out"] for c in range(N_CORES)], axis=0)
    return out.astype(np.float32, copy=False)
